# version 04 (final)
# baseline (speedup 1.0000x reference)
import sys
sys.path.insert(0, "/opt/trn_rl_repo")
import numpy as np
import ml_dtypes
import concourse.bass as bass
import concourse.bacc as bacc
import concourse.mybir as mybir
from concourse.bass_utils import run_bass_kernel_spmd

F32 = mybir.dt.float32
BF16 = mybir.dt.bfloat16
U32 = mybir.dt.uint32
I16 = mybir.dt.int16
AF = mybir.ActivationFunctionType
ALU = mybir.AluOpType
AX = mybir.AxisListType

D = 1024
T = 2048
NCH = D // 128
TG = 512
NTG = T // TG
H, HD = 16, 64
EPS = 1e-5
RBIG = float(3 * 2**22)

N_SUB = 6
DEBUG = False
S2_SCAN = True
S2_FP8 = True
S2_CORR = True
S2_LVL = 9
S2_STOP = 0
L1_STOP = 0


class Sched:
    ENGS = ("pe", "dve", "act", "pool", "sp")

    def __init__(self):
        self.q = {e: [] for e in self.ENGS}
        self.cnt = {e: 0 for e in self.ENGS}
        self.dma = {e: 0 for e in self.ENGS}
        self.cc = 0
        self.buf = {}
        self._bar = None
        self.serialize = False

    def _deps_to_waits(self, deps):
        m = {}
        for t in deps:
            if t is None:
                continue
            k = (t[0], t[1])
            m[k] = max(m.get(k, 0), t[2])
        return m

    def _buf_deps(self, reads, writes):
        deps = []
        for b in reads:
            e = self.buf.get(b)
            if e and e["w"]:
                deps.append(e["w"])
        for b in writes:
            e = self.buf.get(b)
            if e:
                if e["w"]:
                    deps.append(e["w"])
                deps.extend(e["r"])
        return deps

    def _record(self, ticket, reads, writes):
        for b in reads:
            self.buf.setdefault(b, {"w": None, "r": []})["r"].append(ticket)
        for b in writes:
            self.buf[b] = {"w": ticket, "r": []}

    def _push(self, eng, fn, kind, amt, reads, writes, extra):
        if self.serialize:
            self.barrier()
        deps = self._buf_deps(reads, writes) + list(extra)
        if self._bar is not None and eng not in self._bar["done"]:
            deps = deps + self._bar["snap"]
            self._bar["done"].add(eng)
            if len(self._bar["done"]) == len(self.ENGS):
                self._bar = None
        waits = self._deps_to_waits(deps)
        waits.pop(("c", eng), None)
        if kind == "d" and self.dma[eng] > 0:
            waits[("d", eng)] = 16 * self.dma[eng]
        if kind == "c":
            self.cnt[eng] += 1
            ticket = ("c", eng, self.cnt[eng])
        elif kind == "d":
            self.dma[eng] += 1
            ticket = ("d", eng, 16 * self.dma[eng])
        else:
            self.cc += 1
            ticket = ("cc", eng, self.cc)
        self.q[eng].append((waits, fn, (kind, amt)))
        self._record(ticket, reads, writes)
        return ticket

    def barrier(self):
        snap = []
        for e in self.ENGS:
            if self.cnt[e]:
                snap.append(("c", e, self.cnt[e]))
            if self.dma[e]:
                snap.append(("d", e, 16 * self.dma[e]))
        if self.cc:
            snap.append(("cc", "pool", self.cc))
        self._bar = {"snap": snap, "done": set()}

    def emit(self, eng, fn, reads=(), writes=(), extra=()):
        return self._push(eng, fn, "c", 1, reads, writes, extra)

    def dma_emit(self, eng, fn, reads=(), writes=(), extra=()):
        return self._push(eng, fn, "d", 16, reads, writes, extra)

    def cc_emit(self, fn, reads=(), writes=(), extra=()):
        return self._push("pool", fn, "cc", 1, reads, writes, extra)


def build_program():
    nc = bacc.Bacc("TRN2", target_bir_lowering=False)
    sch = Sched()

    def din(name, shape, dt=F32):
        return nc.dram_tensor(name, list(shape), dt, kind="ExternalInput").ap()

    XIN = din("XIN", [D, T])
    FLAGS = din("FLAGS", [128, 8])
    ONES128 = din("ONES128", [128, 128], BF16)
    ONESF = din("ONESF", [1, 128])
    L0K = din("L0K", [D, D])
    L0A = din("L0A", [D, 16])
    L0B = din("L0B", [16, D])
    L0BIAS = din("L0BIAS", [128, NCH])
    W1P = din("W1P", [4096, 1024])
    W2T = din("W2T", [4096, 1024])
    CWPACK = din("CWPACK", [128, 128])
    TMX8 = din("TMX8", [128, NCH])
    TDEC = din("TDEC", [128, NCH])
    BGROW = din("BGROW", [1, D])
    MW1 = din("MW1", [D, 160])
    MW2X = din("MW2X", [5 * 33, D])
    DW1 = din("DW1", [D, 64])
    DW2 = din("DW2", [64, D])
    WQKV = din("WQKV", [5 * D, D])
    CAUSAL = din("CAUSAL", [128, 128], BF16)
    IDEN128 = din("IDEN128", [128, 128], BF16)
    XOUT = nc.dram_tensor("XOUT", [D, T], F32, kind="ExternalOutput").ap()

    from contextlib import ExitStack
    stack = ExitStack()
    e = stack.enter_context

    X = e(nc.sbuf_tensor("X", [128, NCH * T], F32))
    XR16 = e(nc.sbuf_tensor("XR16", [128, NCH * T], BF16))
    XN16 = e(nc.sbuf_tensor("XN16", [128, NCH * T], BF16))
    RSTD = e(nc.sbuf_tensor("RSTD", [128, T], BF16))
    MA = e(nc.sbuf_tensor("MA", [128, T], BF16))
    STAT = e(nc.sbuf_tensor("STAT", [128, TG], F32))
    ONES = e(nc.sbuf_tensor("ONES", [128, 128], BF16))
    ONESF_S = e(nc.sbuf_tensor("ONESF_S", [1, 128], F32))
    FLG = e(nc.sbuf_tensor("FLG", [128, 8], F32))
    LORA_A = e(nc.sbuf_tensor("LORA_A", [128, NCH * 16], BF16))
    LBIAS = e(nc.sbuf_tensor("LBIAS", [128, NCH], F32))
    KQ = e(nc.sbuf_tensor("KQ", [128, 12], F32))
    KSC = e(nc.sbuf_tensor("KSC", [128, 8], F32))
    KMX = e(nc.sbuf_tensor("KMX", [1, 8], F32))
    KQ2 = e(nc.sbuf_tensor("KQ2", [128, 36], F32))
    KSC2 = e(nc.sbuf_tensor("KSC2", [128, 8], F32))
    HQB = e(nc.sbuf_tensor("HQB", [128, 32 * 512], mybir.dt.float16))
    WBUF = HQB[:, 0:8192].bitcast(BF16)
    KSTG = HQB[:, 8192:12288].bitcast(F32)
    WSL = e(nc.sbuf_tensor("WSL", [128, 2048], BF16))
    W2SL = e(nc.sbuf_tensor("W2SL", [128, 2048], BF16))
    HBND = e(nc.sbuf_tensor("HBND", [128, 32], mybir.dt.float16))
    XQH = e(nc.sbuf_tensor("XQH", [128, 20], F32))
    XQHR = e(nc.sbuf_tensor("XQHR", [128, 40], F32))
    XH = e(nc.sbuf_tensor("XH", [128, 8 * 6], BF16))
    XHC = e(nc.sbuf_tensor("XHC", [128, 2], F32))
    CWP = e(nc.sbuf_tensor("CWP", [128, 128], F32))
    FP8 = mybir.dt.float8e4
    TNH1 = e(nc.sbuf_tensor("TNH1", [128, 256], BF16))
    TNH2 = e(nc.sbuf_tensor("TNH2", [128, 256], BF16))
    TNH3 = e(nc.sbuf_tensor("TNH3", [64, 256], BF16))
    MW2B = e(nc.sbuf_tensor("MW2B", [128, D], BF16))
    TH64 = e(nc.sbuf_tensor("TH64", [64, 256], BF16))
    DECB = e(nc.sbuf_tensor("DECB", [128, 2048], BF16))
    GTMB = e(nc.sbuf_tensor("GTMB", [128, 2048], BF16))
    XIB = e(nc.sbuf_tensor("XIB", [128, 2048], BF16))
    GAM = e(nc.sbuf_tensor("GAM", [128, 264], F32))
    EQG = e(nc.sbuf_tensor("EQG", [128, 128], F32))
    QL = e(nc.sbuf_tensor("QL", [128, 128], BF16))
    QG = e(nc.sbuf_tensor("QG", [128, 128], BF16))
    KL = e(nc.sbuf_tensor("KL", [128, 128], BF16))
    KF = e(nc.sbuf_tensor("KF", [128, 128], BF16))
    K2 = e(nc.sbuf_tensor("K2", [128, 128], BF16))
    K2T = e(nc.sbuf_tensor("K2T", [128, 128], BF16))
    ASB = e(nc.sbuf_tensor("ASB", [128, 256], BF16))
    GCOFF = e(nc.sbuf_tensor("GCOFF", [128, NCH], F32))
    EGC = e(nc.sbuf_tensor("EGC", [128, NCH], F32))
    SST = e(nc.sbuf_tensor("SST", [128, 512], F32))
    CAUS = e(nc.sbuf_tensor("CAUS", [128, 128], BF16))
    IDEN = e(nc.sbuf_tensor("IDEN", [128, 128], BF16))
    TMAS = e(nc.sbuf_tensor("TMAS", [128, NCH], F32))
    TDS = e(nc.sbuf_tensor("TDS", [128, NCH], F32))
    DW1S = e(nc.sbuf_tensor("DW1S", [128, 512], BF16))
    BGREP = e(nc.sbuf_tensor("BGREP", [128, D], BF16))
    XPC = e(nc.sbuf_tensor("XPC", [128, NCH], F32))
    XPC16 = e(nc.sbuf_tensor("XPC16", [128, NCH], BF16))
    SCB = e(nc.sbuf_tensor("SCB", [128, 512], BF16))
    EXPG = e(nc.sbuf_tensor("EXPG", [128, 256], BF16))
    QF = e(nc.sbuf_tensor("QF", [128, 64], BF16))
    ONESW = e(nc.sbuf_tensor("ONESW", [128, 256], BF16))
    M1B = XR16[:, 0:2048]
    ABSB = XR16[:, 2048:4096]
    KSTG2 = XR16[:, 4096:8192].bitcast(F32)
    SREP = XR16[:, 8192:12288].bitcast(F32)
    CREP = XR16[:, 12288:16384].bitcast(F32)
    MXR = RSTD
    CREP2 = MA[:, 0:512]
    Q16 = RSTD[:, :]
    K16 = MA[:, :]
    OGTM = XIB[:, :]
    OGFM = GTMB[:, :]
    VTMB = W2SL
    QC16 = HQB[:, :].bitcast(BF16)
    M1ROW = KSTG2[0:1, :]
    TMPR = DECB[0:16, :]
    BGR = SREP[0:1, 0:1024]
    S16D = CWP[:, 0:64].bitcast(BF16)
    CV1 = DECB[:, 0:512].bitcast(mybir.dt.float16)
    HS16 = DECB[:, 512:1024].bitcast(mybir.dt.float16)
    S2R16 = DECB[:, 1024:1536].bitcast(mybir.dt.float16)
    AB2 = DECB[:, 1536:2048].bitcast(mybir.dt.float16)
    HSP = GTMB[:, 0:516].bitcast(mybir.dt.float16)
    SRC16 = DECB[:, 512:1024]
    LORA_B = GTMB[0:16, 0:1024]
    DW2R = MW2B
    DLT = STAT[:, 0:132]
    EPS_T = STAT[:, 132:264]
    SCR1 = STAT[:, 264:396]
    SUPD = STAT[:, 396:460]

    PS = [e(nc.psum_tensor(f"PS{i}", [128, 512], F32)) for i in range(8)]

    sem_c = {eng: e(nc.semaphore(f"s_{eng}")) for eng in ("pe", "dve", "act", "pool")}
    sem_d = {eng: e(nc.semaphore(f"d_{eng}")) for eng in ("pool", "sp", "act")}
    sem_cc = e(nc.semaphore("s_cc"))
    block = e(nc.Block())

    def chunk(buf, c, width=T):
        return buf[:, c * width:(c + 1) * width]

    def pswr(i):
        ent = sch.buf.get(f"ps{i}")
        return ent["w"] if ent else None

    XR_ALIAS = {0: ["M1"], 1: [], 2: ["KSTG2"], 3: ["KSTG2b"], 4: ["SREP"], 5: ["SREPb"],
                6: ["CREP"], 7: ["CREPb"]}

    def ln():
        for c in range(NCH):
            wr = [f"XR{c}"] + XR_ALIAS[c]
            if c % 2 == 0:
                sch.emit("dve", lambda v, c=c: v.tensor_copy(chunk(XR16, c), chunk(X, c)),
                         reads=[f"X{c}"], writes=wr)
            else:
                sch.emit("act", lambda s, c=c: s.copy(chunk(XR16, c), chunk(X, c)),
                         reads=[f"X{c}"], writes=wr)
        for c in range(NCH):
            sch.emit("act", lambda s, c=c: s.square(chunk(XN16, c), chunk(XR16, c)),
                     reads=[f"XR{c}"], writes=[f"XN{c}"])
        for g in range(NTG):
            for c in range(NCH):
                sch.emit("pe", lambda t, g=g, c=c: t.matmul(
                    PS[g][:, :], ONES[:, :], XR16[:, c * T + g * TG: c * T + (g + 1) * TG],
                    start=(c == 0), stop=(c == NCH - 1)),
                    reads=[f"XR{c}", "ONES"],
                    writes=[f"ps{g}"] if c in (0, NCH - 1) else [])
            for c in range(NCH):
                sch.emit("pe", lambda t, g=g, c=c: t.matmul(
                    PS[4 + g][:, :], ONES[:, :], XN16[:, c * T + g * TG: c * T + (g + 1) * TG],
                    start=(c == 0), stop=(c == NCH - 1)),
                    reads=[f"XN{c}", "ONES"],
                    writes=[f"ps{4+g}"] if c in (0, NCH - 1) else [])
        for g in range(NTG):
            sl = slice(g * TG, (g + 1) * TG)
            sch.emit("dve", lambda v, g=g, sl=sl: v.tensor_scalar(
                MA[:, sl], PS[g][:, :], 1.0 / D, None, ALU.mult),
                reads=[f"ps{g}"], writes=[f"MAg{g}", "CREP2", "M2", "OGTM", "OGFM"] if g == 0 else [f"MAg{g}"])
            sch.emit("dve", lambda v, g=g, sl=sl: v.tensor_scalar(
                RSTD[:, sl], PS[4 + g][:, :], 1.0 / D, None, ALU.mult),
                reads=[f"ps{4+g}"], writes=[f"RSg{g}", "MXR", "MXR2", "Q16", "K16"] if g == 0 else [f"RSg{g}"])
            sch.emit("dve", lambda v, sl=sl: v.tensor_tensor(
                STAT[:, :], MA[:, sl], MA[:, sl], ALU.mult),
                reads=[f"MAg{g}"], writes=["STAT", "DLT", "EPS", "SCR1"])
            sch.emit("dve", lambda v, sl=sl: v.tensor_sub(
                RSTD[:, sl], RSTD[:, sl], STAT[:, :]),
                reads=["STAT", f"RSg{g}"], writes=[f"RSg{g}"])
            sch.emit("act", lambda s, sl=sl: s.activation(
                RSTD[:, sl], RSTD[:, sl], AF.Ln, bias=FLG[:, 2:3], scale=1.0),
                reads=[f"RSg{g}", "FLG"], writes=[f"RSg{g}"])
            sch.emit("act", lambda s, sl=sl: s.activation(
                RSTD[:, sl], RSTD[:, sl], AF.Exp, scale=-0.5),
                reads=[f"RSg{g}"], writes=[f"RSg{g}"])
            sch.emit("dve", lambda v, sl=sl: v.scalar_tensor_tensor(
                MA[:, sl], MA[:, sl], -1.0, RSTD[:, sl], ALU.mult, ALU.mult),
                reads=[f"MAg{g}", f"RSg{g}"], writes=[f"MAg{g}"])
        rs_deps = [f"RSg{g}" for g in range(NTG)]
        ma_deps = [f"MAg{g}" for g in range(NTG)]
        for c in range(NCH):
            eng = "dve"
            sch.emit(eng, lambda v, c=c: v.tensor_tensor(
                chunk(XN16, c), chunk(XR16, c), RSTD[:, :], ALU.mult),
                reads=[f"XR{c}"] + rs_deps, writes=[f"XN{c}"])
        for c in range(NCH):
            eng = "dve"
            sch.emit(eng, lambda v, c=c: v.tensor_tensor(
                chunk(XN16, c), chunk(XN16, c), MA[:, :], ALU.add),
                reads=[f"XN{c}"] + ma_deps, writes=[f"XN{c}"])

    def l0ffn_prep():
        for cf in range(NCH):
            sch.dma_emit("pool", lambda g, cf=cf: g.dma_start(
                out=LORA_A[:, cf * 16:(cf + 1) * 16], in_=L0A[cf * 128:(cf + 1) * 128, :]),
                writes=[f"LA{cf}"])
        sch.dma_emit("pool", lambda g: g.dma_start(out=LORA_B[:, :], in_=L0B), writes=["LB", "GTMB", "OGFM"])
        sch.dma_emit("sp", lambda s: s.dma_start(out=LBIAS[:, :], in_=L0BIAS), writes=["LBIAS"])
        for cf in range(NCH):
            sl = KSTG[:, (cf % 2) * 1024:(cf % 2) * 1024 + 1024]
            sch.dma_emit("sp", lambda s, cf=cf, sl=sl: s.dma_start(
                out=sl, in_=L0K[cf * 128:(cf + 1) * 128, :]), writes=[f"KS{cf % 2}"])
            sch.emit("dve", lambda v, cf=cf, sl=sl: v.reduce_max(
                KQ[:, cf:cf + 1], sl, AX.X, apply_absolute_value=True),
                reads=[f"KS{cf % 2}"], writes=[f"KQ{cf}"])
        sch.emit("dve", lambda v: v.reduce_max(KQ[:, 8:9], KQ[:, 0:8], AX.X),
                 reads=[f"KQ{c}" for c in range(NCH)], writes=["KQr"])
        sch.emit("pool", lambda g: g.tensor_reduce(KMX[0:1, 0:1], KQ[:, 8:9], AX.C, ALU.max),
                 reads=["KQr"], writes=["KM0"])
        sch.emit("pool", lambda g: g.tensor_copy(KMX[0:1, 4:5], KMX[0:1, 0:1]),
                 reads=["KM0"], writes=["KMf"])
        sch.emit("pool", lambda g: g.tensor_copy(KMX[0:1, 5:6], KMX[0:1, 4:5]),
                 reads=["KMf"], writes=["KM"])
        sch.emit("dve", lambda v: v.tensor_scalar(KSC[0:1, 2:3], KMX[0:1, 0:1], 1.0 / 7.0, None, ALU.mult),
                 reads=["KM"], writes=["KM"])
        sch.emit("dve", lambda v: v.tensor_scalar(KSC[0:1, 3:4], KMX[0:1, 0:1], 1.0 / 7.0, None, ALU.mult),
                 reads=["KM"], writes=["KM"])
        sch.emit("pe", lambda t: t.matmul(PS[7][:, 0:2], ONESF_S[:, :], KSC[0:1, 2:4],
                                          start=True, stop=True),
                 reads=["KM", "ONESF"], writes=["ps7"])
        sch.emit("dve", lambda v: v.tensor_copy(KSC[:, 4:6], PS[7][:, 0:2]),
                 reads=["ps7"], writes=["KB"])
        sch.emit("act", lambda a: a.activation(KSC[:, 6:7], KSC[:, 4:5], AF.Ln),
                 reads=["KB"], writes=["KBl"])
        sch.emit("act", lambda a: a.activation(KSC[:, 6:7], KSC[:, 6:7], AF.Exp, scale=-1.0),
                 reads=["KBl"], writes=["KB"])
        for cf in range(NCH):
            sl = KSTG[:, (cf % 2) * 1024:(cf % 2) * 1024 + 1024]
            sch.dma_emit("sp", lambda s, cf=cf, sl=sl: s.dma_start(
                out=sl, in_=L0K[cf * 128:(cf + 1) * 128, :]), writes=[f"KS{cf % 2}"])
            sch.emit("dve", lambda v, sl=sl: v.tensor_scalar(
                sl, sl, KSC[:, 6:7], None, ALU.mult), reads=[f"KS{cf % 2}", "KB"], writes=[f"KS{cf % 2}"])
            sch.emit("dve", lambda v, sl=sl: v.tensor_scalar(
                sl, sl, RBIG, -RBIG, ALU.add, ALU.add), reads=[f"KS{cf % 2}"], writes=[f"KS{cf % 2}"])
            sch.emit("dve", lambda v, sl=sl: v.tensor_scalar(
                sl, sl, -8.0, 7.0, ALU.max, ALU.min), reads=[f"KS{cf % 2}"], writes=[f"KS{cf % 2}"])
            sch.emit("dve", lambda v, cf=cf, sl=sl: v.tensor_scalar(
                WBUF[:, cf * 1024:(cf + 1) * 1024], sl, KSC[:, 4:5], None, ALU.mult),
                reads=[f"KS{cf % 2}", "KB"], writes=[f"W{cf}"])

    def l0ffn():
        ln()
        for g in range(NTG):
            for cf in range(NCH):
                sch.emit("pe", lambda t, g=g, cf=cf: t.matmul(
                    PS[6][0:16, 0:TG], LORA_A[:, cf * 16:(cf + 1) * 16],
                    XN16[:, cf * T + g * TG: cf * T + (g + 1) * TG],
                    start=(cf == 0), stop=(cf == NCH - 1)),
                    reads=[f"XN{cf}", f"LA{cf}"],
                    writes=["ps6"] if cf in (0, NCH - 1) else [])
            sch.emit("dve", lambda v, g=g: v.tensor_copy(TMPR[:, g * TG:(g + 1) * TG], PS[6][0:16, 0:TG]),
                     reads=["ps6"], writes=[f"TMPR{g}", "DECB"] if g == 0 else [f"TMPR{g}"])
        for co in range(NCH):
            for g in range(NTG):
                pb = (co * NTG + g) % 2
                for cf in range(NCH):
                    sch.emit("pe", lambda t, co=co, g=g, cf=cf, pb=pb: t.matmul(
                        PS[pb][:, 0:TG],
                        WBUF[:, cf * 1024 + co * 128: cf * 1024 + (co + 1) * 128],
                        XN16[:, cf * T + g * TG: cf * T + (g + 1) * TG],
                        start=(cf == 0), stop=False),
                        reads=[f"XN{cf}", f"L0K16c{cf}"],
                        writes=[f"ps{pb}"] if cf == 0 else [])
                sch.emit("pe", lambda t, co=co, g=g, pb=pb: t.matmul(
                    PS[pb][:, 0:TG], LORA_B[:, co * 128:(co + 1) * 128],
                    TMPR[:, g * TG:(g + 1) * TG], start=False, stop=True),
                    reads=[f"TMPR{g}", "LB", "GTMB"], writes=[f"ps{pb}"])
                sch.emit("dve", lambda v, co=co, pb=pb: v.tensor_scalar(
                    STAT[:, :], PS[pb][:, 0:TG], LBIAS[:, co:co + 1], 0.5, ALU.add, ALU.mult),
                    reads=[f"ps{pb}", "LBIAS"], writes=["STAT"])
                sch.emit("dve", lambda v, co=co, g=g: v.tensor_tensor(
                    X[:, co * T + g * TG: co * T + (g + 1) * TG],
                    X[:, co * T + g * TG: co * T + (g + 1) * TG],
                    STAT[:, :], ALU.add),
                    reads=["STAT", f"X{co}"], writes=[f"X{co}"])

    W1S = nc.dram_tensor("W1S", [4096, 1024], BF16)
    WQ2T = nc.dram_tensor("WQ2T", [4096, D], BF16)
    HALO_IN = nc.dram_tensor("HALO_IN", [128, 20], F32)
    HALO_OUT = nc.dram_tensor("HALO_OUT", [256, 20], F32)
    M2B = MA[:, 512:1024]
    NTAU = 4
    TAUW = 512

    def pool_reduce_fenced(out_ap, in_ap, op, key_in, key_out, scr):
        sch.emit("pool", lambda g: g.tensor_reduce(out_ap, in_ap, AX.C, op),
                 reads=[key_in], writes=[key_out + "0"])
        sch.emit("pool", lambda g: g.tensor_copy(scr, out_ap[0:1, 0:1]),
                 reads=[key_out + "0"], writes=[key_out + "f"])
        sch.emit("pool", lambda g: g.tensor_copy(scr, scr),
                 reads=[key_out + "f"], writes=[key_out])

    def bcast128(dst, src_row, key_in, key_out, nbank=7):
        n = dst.shape[1]
        sch.emit("pe", lambda t: t.matmul(PS[nbank][:, 0:n], ONESF_S[:, :], src_row,
                                          start=True, stop=True),
                 reads=[key_in, "ONESF"], writes=[f"ps{nbank}"])
        sch.emit("dve", lambda v: v.tensor_copy(dst, PS[nbank][:, 0:n]),
                 reads=[f"ps{nbank}"], writes=[key_out])

    def l1ffn_prep():
        for wi, (WT, WQT) in enumerate([(W1P, W1S), (W2T, WQ2T)]):
            for i in range(32):
                sl = KSTG[:, (i % 2) * 1024:(i % 2) * 1024 + 1024]
                sch.dma_emit("sp", lambda s, i=i, sl=sl, WT=WT: s.dma_start(
                    out=sl, in_=WT[i * 128:(i + 1) * 128, :]), writes=[f"KS{i % 2}"])
                sch.emit("dve", lambda v, sl=sl, i=i: v.tensor_reduce(
                    KQ2[:, i:i + 1], sl, AX.X, ALU.add, apply_absolute_value=True),
                    reads=[f"KS{i % 2}"], writes=[f"KQ2_{i}"])
            sch.emit("dve", lambda v: v.tensor_reduce(KQ2[:, 32:33], KQ2[:, 0:32], AX.X, ALU.add),
                     reads=[f"KQ2_{i}" for i in range(32)], writes=["KQ2r"])
            pool_reduce_fenced(KMX[0:1, 0:1], KQ2[:, 32:33], ALU.add, "KQ2r", f"MWsum{wi}", KMX[0:1, 6:7])
            sch.emit("dve", lambda v: v.tensor_scalar(
                KSC[0:1, 2:3], KMX[0:1, 0:1], 1.0 / 4194304.0, None, ALU.mult),
                reads=[f"MWsum{wi}"], writes=[f"MWm{wi}"])
            sch.emit("dve", lambda v: v.tensor_scalar(
                KSC[0:1, 3:4], KMX[0:1, 0:1], 1.0 / 4194304.0, None, ALU.mult),
                reads=[f"MWm{wi}"], writes=[f"MWm{wi}"])
            sch.emit("dve", lambda v: v.tensor_scalar(KSC[0:1, 2:4], KSC[0:1, 2:4], FLG[0:1, 2:3], None, ALU.max),
                     reads=[f"MWm{wi}", "FLG"], writes=[f"MWm{wi}"])
            bcast128(KSC2[:, 2 * wi:2 * wi + 2], KSC[0:1, 2:4], f"MWm{wi}", f"MWb{wi}")
            sch.emit("act", lambda a, wi=wi: a.activation(KSC2[:, 4 + wi:5 + wi], KSC2[:, 2 * wi:2 * wi + 1], AF.Ln),
                     reads=[f"MWb{wi}"], writes=[f"MWl{wi}"])
            sch.emit("act", lambda a, wi=wi: a.activation(KSC2[:, 4 + wi:5 + wi], KSC2[:, 4 + wi:5 + wi], AF.Exp, scale=-1.0),
                     reads=[f"MWl{wi}"], writes=[f"MWi{wi}"])
            for i in range(32):
                sl = KSTG[:, (i % 2) * 1024:(i % 2) * 1024 + 1024]
                w16 = WSL[:, (i % 2) * 1024:(i % 2) * 1024 + 1024]
                sch.dma_emit("sp", lambda s, i=i, sl=sl, WT=WT: s.dma_start(
                    out=sl, in_=WT[i * 128:(i + 1) * 128, :]), writes=[f"KS{i % 2}"])
                sch.emit("dve", lambda v, sl=sl, wi=wi: v.tensor_scalar(
                    sl, sl, KSC2[:, 4 + wi:5 + wi], None, ALU.mult),
                    reads=[f"KS{i % 2}", f"MWi{wi}"], writes=[f"KS{i % 2}"])
                sch.emit("dve", lambda v, sl=sl: v.tensor_scalar(
                    sl, sl, RBIG, -RBIG, ALU.add, ALU.add),
                    reads=[f"KS{i % 2}"], writes=[f"KS{i % 2}"])
                sch.emit("dve", lambda v, sl=sl, w16=w16: v.tensor_scalar(
                    w16, sl, -1.0, 1.0, ALU.max, ALU.min),
                    reads=[f"KS{i % 2}"], writes=[f"WSLq{i % 2}"])
                sch.dma_emit("act", lambda a, i=i, w16=w16, WQT=WQT: a.dma_start(
                    out=WQT[i * 128:(i + 1) * 128, :], in_=w16),
                    reads=[f"WSLq{i % 2}"], writes=[])

    def l1ffn(it):
        ln()
        sch.emit("act", lambda a: a.activation(M1B[:, :], chunk(XN16, 0), AF.Abs),
                 reads=["XN0"], writes=["M1", "XR0"])
        for c in range(1, NCH):
            sch.emit("act", lambda a, c=c: a.activation(ABSB[:, :], chunk(XN16, c), AF.Abs),
                     reads=[f"XN{c}"], writes=["ABSB", "XR1"] if c == 1 else ["ABSB"])
            sch.emit("dve", lambda v: v.tensor_tensor(M1B[:, :], M1B[:, :], ABSB[:, :], ALU.max),
                     reads=["M1", "ABSB"], writes=["M1"])
        pool_reduce_fenced(M1ROW[0:1, :], M1B[:, :], ALU.max, "M1", "M1r", KMX[0:1, 7:8])
        sch.buf.setdefault("KSTG2", {"w": None, "r": []})["r"].append(("c", "pool", sch.cnt["pool"]))
        for g in range(NTG):
            sch.emit("pe", lambda t, g=g: t.matmul(PS[g][:, :], ONESF_S[:, :],
                                                   M1ROW[0:1, g * TG:(g + 1) * TG], start=True, stop=True),
                     reads=["M1r", "ONESF"], writes=[f"ps{g}"])
            sch.emit("dve", lambda v, g=g: v.tensor_scalar(
                MXR[:, g * TG:(g + 1) * TG], PS[g][:, :], FLG[:, 2:3], None, ALU.max),
                reads=[f"ps{g}", "FLG"], writes=["MXR", "RSg0"] if g == 0 else ["MXR"])
        sch.emit("act", lambda a: a.activation(SREP[:, :], MXR[:, :], AF.Ln),
                 reads=["MXR"], writes=["SREP", "XR4", "XR5"])
        sch.emit("act", lambda a: a.activation(SREP[:, :], SREP[:, :], AF.Exp, scale=-1.0, bias=FLG[:, 4:5]),
                 reads=["SREP", "FLG"], writes=["SREP"])
        sch.emit("dve", lambda v: v.tensor_scalar(CREP[:, :], MXR[:, :], KSC2[:, 0:1], 1.0 / 127.0, ALU.mult, ALU.mult),
                 reads=["MXR", "MWb0"], writes=["CREP", "XR6", "XR7"])
        for c in range(NCH):
            sch.emit("dve", lambda v, c=c: v.tensor_tensor(KSTG2[:, :], chunk(XN16, c), SREP[:, :], ALU.mult),
                     reads=[f"XN{c}", "SREP"], writes=["KSTG2", "XR2", "XR3", "M1r", "M2r"] if c == 0 else ["KSTG2"])
            sch.emit("dve", lambda v, c=c: v.tensor_scalar(chunk(XN16, c), KSTG2[:, :], RBIG, -RBIG, ALU.add, ALU.add),
                     reads=["KSTG2"], writes=[f"XN{c}"])
        for c in range(NCH):
            sch.emit("dve", lambda v, c=c: v.tensor_copy(XQH[:, c:c + 1], XN16[:, c * T:c * T + 1]),
                     reads=[f"XN{c}"], writes=["XQH"])
            sch.emit("dve", lambda v, c=c: v.tensor_copy(XQH[:, 8 + c:9 + c], XN16[:, c * T + T - 1:c * T + T]),
                     reads=[f"XN{c}"], writes=["XQH"])
        sch.emit("dve", lambda v: v.tensor_copy(XQH[:, 16:17], CREP[:, 0:1]), reads=["CREP"], writes=["XQH"])
        sch.emit("dve", lambda v: v.tensor_copy(XQH[:, 17:18], CREP[:, T - 1:T]), reads=["CREP"], writes=["XQH"])
        sch.dma_emit("pool", lambda g: g.dma_start(out=HALO_IN.ap(), in_=XQH[:, :]),
                     reads=["XQH"], writes=["HIN"])
        sch.cc_emit(lambda g: g.collective_compute(
            "AllGather", ALU.bypass, replica_groups=[[0, 1], [2, 3], [4, 5], [6, 7]],
            ins=[HALO_IN.ap()], outs=[HALO_OUT.ap()]),
            reads=["HIN"], writes=["HOUT"])
        sch.dma_emit("pool", lambda g: g.dma_start(out=XQHR[:, 0:20], in_=HALO_OUT.ap()[0:128, :]),
                     reads=["HOUT"], writes=["XQHR"])
        sch.dma_emit("pool", lambda g: g.dma_start(out=XQHR[:, 20:40], in_=HALO_OUT.ap()[128:256, :]),
                     reads=["HOUT"], writes=["XQHR"])
        for c in range(NCH):
            sch.emit("dve", lambda v, c=c: v.tensor_scalar(
                XH[:, 6 * c:6 * c + 1], XQHR[:, 8 + c:9 + c], FLG[:, 0:1], None, ALU.mult),
                reads=["XQHR", "FLG"], writes=["XH"])
            sch.emit("dve", lambda v, c=c: v.tensor_scalar(
                XH[:, 6 * c + 1:6 * c + 2], XQHR[:, 20 + c:21 + c], FLG[:, 1:2], None, ALU.mult),
                reads=["XQHR", "FLG"], writes=["XH"])
            for j, tb in enumerate((512, 1024, 1536)):
                sch.emit("dve", lambda v, c=c, j=j, tb=tb: v.tensor_copy(
                    XH[:, 6 * c + 2 + j:6 * c + 3 + j], XN16[:, c * T + tb:c * T + tb + 1]),
                    reads=[f"XN{c}"], writes=["XH"])
        sch.emit("dve", lambda v: v.tensor_scalar(XHC[:, 0:1], XQHR[:, 17:18], FLG[:, 0:1], None, ALU.mult),
                 reads=["XQHR", "FLG"], writes=["XHC"])
        sch.emit("dve", lambda v: v.tensor_scalar(XHC[:, 1:2], XQHR[:, 20 + 16:20 + 17], FLG[:, 1:2], None, ALU.mult),
                 reads=["XQHR", "FLG"], writes=["XHC"])

        for tau in range(NTAU):
            t0 = tau * TAUW
            sch.emit("dve", lambda v: v.memset(M2B[:, :], 0.0), writes=["M2"])
            for ch in range(32):
                wsl = WSL[:, (ch % 2) * 1024:(ch % 2) * 1024 + 1024]
                sch.dma_emit("sp", lambda s, ch=ch, wsl=wsl: s.dma_start(
                    out=wsl, in_=W1S.ap()[ch * 128:(ch + 1) * 128, :]),
                    writes=[f"WSL{ch % 2}"], extra=[("c", "dve", sch.cnt["dve"])])
                pb = (ch % 2) * 2
                for cf in range(NCH):
                    sch.emit("pe", lambda t, t0=t0, cf=cf, wsl=wsl, pb=pb: t.matmul(
                        PS[pb][:, :], wsl[:, cf * 128:(cf + 1) * 128],
                        XN16[:, cf * T + t0: cf * T + t0 + TAUW],
                        start=(cf == 0), stop=(cf == NCH - 1)),
                        reads=[f"XN{cf}", f"WSL{ch % 2}"],
                        writes=[f"ps{pb}"] if cf in (0, NCH - 1) else [])
                for cf in range(NCH):
                    sch.emit("pe", lambda t, cf=cf, wsl=wsl, pb=pb: t.matmul(
                        PS[pb + 1][:, 0:6], wsl[:, cf * 128:(cf + 1) * 128], XH[:, 6 * cf:6 * cf + 6],
                        start=(cf == 0), stop=(cf == NCH - 1)),
                        reads=["XH", f"WSL{ch % 2}"],
                        writes=[f"ps{pb + 1}"] if cf in (0, NCH - 1) else [])
                hs = HSP[:, 0:514]
                sch.emit("dve", lambda v, hs=hs, pb=pb, t0=t0: v.tensor_tensor(
                    hs[:, 1:513], PS[pb][:, :], CREP[:, t0:t0 + TAUW], ALU.mult),
                    reads=[f"ps{pb}", "CREP"], writes=["HSx", "GTMB"])
                if tau == 0:
                    sch.emit("dve", lambda v, hs=hs, pb=pb: v.tensor_scalar(
                        hs[:, 0:1], PS[pb + 1][:, 0:1], XHC[:, 0:1], None, ALU.mult),
                        reads=[f"ps{pb + 1}", "XHC"], writes=["HSx"])
                else:
                    sch.emit("dve", lambda v, hs=hs, ch=ch: v.tensor_copy(
                        hs[:, 0:1], HBND[:, ch:ch + 1]),
                        reads=[f"HBND{ch}"], writes=["HSx"])
                if tau == NTAU - 1:
                    sch.emit("dve", lambda v, hs=hs, pb=pb: v.tensor_scalar(
                        hs[:, 513:514], PS[pb + 1][:, 1:2], XHC[:, 1:2], None, ALU.mult),
                        reads=[f"ps{pb + 1}", "XHC"], writes=["HSx"])
                else:
                    sch.emit("dve", lambda v, hs=hs, pb=pb, tau=tau, t0=t0: v.tensor_scalar(
                        hs[:, 513:514], PS[pb + 1][:, 2 + tau:3 + tau], CREP[:, t0 + TAUW:t0 + TAUW + 1], None, ALU.mult),
                        reads=[f"ps{pb + 1}", "CREP"], writes=["HSx"])
                if tau < NTAU - 1:
                    sch.emit("dve", lambda v, ch=ch, pb=pb, t0=t0: v.tensor_scalar(
                        HBND[:, ch:ch + 1], PS[pb][:, TAUW - 1:TAUW], CREP[:, t0 + TAUW - 1:t0 + TAUW], None, ALU.mult),
                        reads=[f"ps{pb}", "CREP"], writes=[f"HBND{ch}"])
                hq = HQB[:, ch * 512:(ch + 1) * 512]
                sch.emit("dve", lambda v, ch=ch, hs=hs: v.tensor_scalar(
                    CV1[:, :], hs[:, 2:514], CWP[:, 4 * ch + 2:4 * ch + 3], None, ALU.mult),
                    reads=["HSx", "CWP"], writes=["CV1", "DECB"] + [f"TMPR{g}" for g in range(4)])
                sch.emit("dve", lambda v, ch=ch, hs=hs: v.scalar_tensor_tensor(
                    CV1[:, :], hs[:, 0:512], CWP[:, 4 * ch:4 * ch + 1], CV1[:, :], ALU.mult, ALU.add),
                    reads=["HSx", "CWP", "CV1"], writes=["CV1"])
                sch.emit("dve", lambda v, ch=ch, hs=hs: v.scalar_tensor_tensor(
                    CV1[:, :], hs[:, 1:513], CWP[:, 4 * ch + 1:4 * ch + 2], CV1[:, :], ALU.mult, ALU.add),
                    reads=["HSx", "CWP", "CV1"], writes=["CV1"])
                extra_w = (([f"W{ch // 2}", "QC16"] if ch < 16 else
                            (["KS0", "KS1", "QC16"] if ch < 24 else ["QC16"])))
                sch.emit("act", lambda a, ch=ch, hq=hq: a.activation(
                    hq, CV1[:, :], AF.Silu, bias=CWP[:, 4 * ch + 3:4 * ch + 4]),
                    reads=["CV1", "CWP"], writes=[f"HQ{ch}"] + extra_w)
                sch.emit("act", lambda a, hq=hq: a.activation(AB2[:, :], hq, AF.Abs),
                    reads=[f"HQ{ch}"], writes=["AB2"])
                sch.emit("dve", lambda v: v.tensor_tensor(
                    M2B[:, :], M2B[:, :], AB2[:, :], ALU.max),
                    reads=["M2", "AB2"], writes=["M2"])
            pool_reduce_fenced(M1ROW[0:1, 0:TAUW], M2B[:, :], ALU.max, "M2", "M2r", KMX[0:1, 7:8])
            sch.buf.setdefault("KSTG2", {"w": None, "r": []})["r"].append(("c", "pool", sch.cnt["pool"]))
            sch.emit("pe", lambda t: t.matmul(PS[6][:, :], ONESF_S[:, :],
                                              M1ROW[0:1, 0:TAUW], start=True, stop=True),
                     reads=["M2r", "ONESF"], writes=["ps6"])
            sch.emit("dve", lambda v: v.tensor_scalar(
                MXR[:, 0:TAUW], PS[6][:, :], FLG[:, 2:3], None, ALU.max),
                reads=["ps6", "FLG"], writes=["MXR2", "MXR"])
            sch.emit("act", lambda a: a.activation(M1ROW[0:1, 0:TAUW] if False else MXR[:, 512:1024], MXR[:, 0:TAUW], AF.Ln),
                     reads=["MXR2"], writes=["SREP2"])
            sch.emit("act", lambda a: a.activation(MXR[:, 512:1024], MXR[:, 512:1024], AF.Exp, scale=-1.0, bias=FLG[:, 4:5]),
                     reads=["SREP2", "FLG"], writes=["SREP2"])
            sch.emit("dve", lambda v: v.tensor_scalar(
                CREP2[:, :], MXR[:, 0:TAUW], KSC2[:, 2:3], 0.5 / 127.0, ALU.mult, ALU.mult),
                reads=["MXR2", "MWb1"], writes=["CREP2", "MAg0"])
            sch.emit("dve", lambda v: v.tensor_copy(S2R16[:, :], MXR[:, 512:1024]),
                     reads=["SREP2"], writes=["S2R16"])
            for ch in range(32):
                hq = HQB[:, ch * 512:(ch + 1) * 512]
                sch.emit("dve", lambda v, hq=hq: v.tensor_tensor(
                    HS16[:, :], hq, S2R16[:, :], ALU.mult),
                    reads=[f"HQ{ch}", "S2R16"], writes=["HS16"])
                sch.emit("dve", lambda v, hq=hq: v.tensor_scalar(
                    hq, HS16[:, :], RBIG, -RBIG, ALU.add, ALU.add),
                    reads=["HS16"], writes=[f"HQ{ch}"])
            for ch in range(32):
                w2sl = W2SL[:, (ch % 2) * 1024:(ch % 2) * 1024 + 1024]
                sch.dma_emit("act", lambda a, ch=ch, w2sl=w2sl: a.dma_start(
                    out=w2sl, in_=WQ2T.ap()[ch * 128:(ch + 1) * 128, :]),
                    writes=[f"W2SL{ch % 2}"], extra=[("c", "pe", sch.cnt["pe"])])
                for dc in range(NCH):
                    sch.emit("pe", lambda t, ch=ch, dc=dc, w2sl=w2sl: t.matmul(
                        PS[dc][:, :], w2sl[:, dc * 128:(dc + 1) * 128],
                        HQB[:, ch * 512:(ch + 1) * 512],
                        start=(ch == 0), stop=(ch == 31)),
                        reads=[f"HQ{ch}", f"W2SL{ch % 2}"],
                        writes=[f"ps{dc}"] if ch in (0, 31) else [])
            if DEBUG and tau == NTAU - 1 and it == 0:
                DPS = nc.dram_tensor("DPS", [1024, 512], F32, kind="ExternalOutput").ap()
                for dc in range(NCH):
                    sch.dma_emit("sp", lambda sp_, dc=dc: sp_.dma_start(
                        out=DPS[dc * 128:(dc + 1) * 128, :], in_=PS[dc][:, :]),
                        reads=[f"ps{dc}"]) if False else None
                for dc in range(NCH):
                    sch.emit("dve", lambda v, dc=dc: v.tensor_copy(KSTG2[:, 0:TAUW], PS[dc][:, :]),
                             reads=[f"ps{dc}"], writes=["KSTG2"])
                    sch.dma_emit("sp", lambda sp_, dc=dc: sp_.dma_start(
                        out=DPS[dc * 128:(dc + 1) * 128, :], in_=KSTG2[:, 0:TAUW]),
                        reads=["KSTG2"])
            for dc in range(NCH):
                sch.emit("dve", lambda v, dc=dc: v.tensor_tensor(
                    KSTG2[:, 0:TAUW], PS[dc][:, :], CREP2[:, :], ALU.mult),
                    reads=[f"ps{dc}", "CREP2"], writes=["KSTG2"])
                if DEBUG and tau == NTAU - 1 and it == 0:
                    DDL = nc.dram_tensor("DDL", [1024, 512], F32, kind="ExternalOutput").ap() if dc == 0 else DDLh[0]
                    if dc == 0:
                        DDLh = [DDL]
                    sch.dma_emit("sp", lambda sp_, dc=dc, DDL=DDL: sp_.dma_start(
                        out=DDL[dc * 128:(dc + 1) * 128, :], in_=KSTG2[:, 0:TAUW]),
                        reads=["KSTG2"])
                sch.emit("dve", lambda v, dc=dc, t0=t0: v.tensor_tensor(
                    X[:, dc * T + t0: dc * T + t0 + TAUW],
                    X[:, dc * T + t0: dc * T + t0 + TAUW], KSTG2[:, 0:TAUW], ALU.add),
                    reads=["KSTG2", f"X{dc}"], writes=[f"X{dc}"])

    CC2_IN = nc.dram_tensor("CC2_IN", [128, 512], F32)
    CC2_OUT = nc.dram_tensor("CC2_OUT", [256, 512], F32)
    GAD = nc.dram_tensor("GAD", [128, 16384], BF16)
    SC = 256
    NSC = T // SC
    MW1S = WSL

    def psum_half(idx):
        return PS[idx // 2][:, (idx % 2) * 256:(idx % 2) * 256 + 256]

    def l0attn():
        ln()
        sch.serialize = True
        if DEBUG:
            DX0 = nc.dram_tensor("DX0", [128, 256], F32, kind="ExternalOutput").ap()
            sch.dma_emit("sp", lambda s: s.dma_start(out=DX0, in_=X[:, 0:256]), reads=["X0"])
        if S2_STOP == 1:
            return
        if sch.buf.get("S2INIT") is None:
            sch.dma_emit("sp", lambda s: s.dma_start(out=TMAS[:, :], in_=TMX8), writes=["TMAS"])
            sch.dma_emit("sp", lambda s: s.dma_start(out=IDEN[:, :], in_=IDEN128), writes=["IDEN"])
            sch.dma_emit("sp", lambda s: s.dma_start(out=TDS[:, :], in_=TDEC), writes=["TDS"])
            sch.dma_emit("sp", lambda s: s.dma_start(out=CAUS[:, :], in_=CAUSAL), writes=["CAUS"])
            sch.dma_emit("pool", lambda g: g.dma_start(out=DW1S[:, :].rearrange("p (a b) -> p a b", a=NCH),
                                                       in_=DW1.rearrange("(a p) c -> p a c", p=128)),
                         writes=["DW1S"])

            sch.dma_emit("sp", lambda s: s.dma_start(out=BGR[0:1, :], in_=BGROW),
                         writes=["BGR0", "XR4", "SREP"])
            for gg in range(2):
                sch.emit("pe", lambda t, gg=gg: t.matmul(PS[6][:, :], ONESF_S[:, :],
                                                         BGR[0:1, gg * 512:(gg + 1) * 512], start=True, stop=True),
                         reads=["BGR0", "ONESF"], writes=["ps6"])
                sch.emit("dve", lambda v, gg=gg: v.tensor_copy(BGREP[:, gg * 512:(gg + 1) * 512], PS[6][:, :]),
                         reads=["ps6"], writes=["BGREP"])
            sch.emit("dve", lambda v: v.memset(SST[:, :], 0.0), writes=["SST"])
            sch.emit("dve", lambda v: v.memset(S16D[:, :], 0.0), writes=["S16D"])
            sch.emit("dve", lambda v: v.memset(ONESW[:, :], 1.0), writes=["ONESW"])
            sch.emit("dve", lambda v: v.memset(GCOFF[:, :], 0.0), writes=["GCOFF"])
            sch.emit("dve", lambda v: v.memset(PS[4][:, :], 0.0), writes=["ps4"])
            sch.emit("dve", lambda v: v.memset(PS[5][:, :], 0.0), writes=["ps5"])
            for tn, base in [(TNH1, 32), (TNH1, 96), (TNH2, 32), (TNH2, 96), (TNH3, 32)]:
                sch.emit("dve", lambda v, tn=tn, base=base: v.memset(tn[base:base + 1, :], 1.0),
                         writes=["TNHones"])
            sch.buf["S2INIT"] = sch.buf["TNHones"]
        for c in range(NCH):
            sch.emit("dve", lambda v, c=c: v.tensor_copy(XQH[:, c:c + 1], XN16[:, c * T + T - 1:c * T + T]),
                     reads=[f"XN{c}"], writes=["XQH"])
        sch.dma_emit("pool", lambda g: g.dma_start(out=HALO_IN.ap(), in_=XQH[:, :]),
                     reads=["XQH"], writes=["HIN"])
        sch.cc_emit(lambda g: g.collective_compute(
            "AllGather", ALU.bypass, replica_groups=[[0, 1], [2, 3], [4, 5], [6, 7]],
            ins=[HALO_IN.ap()], outs=[HALO_OUT.ap()]),
            reads=["HIN"], writes=["HOUT"])
        sch.dma_emit("pool", lambda g: g.dma_start(out=XQHR[:, 0:20], in_=HALO_OUT.ap()[0:128, :]),
                     reads=["HOUT"], writes=["XQHR"])
        for c in range(NCH):
            sch.emit("dve", lambda v, c=c: v.tensor_scalar(
                XPC[:, c:c + 1], XQHR[:, c:c + 1], FLG[:, 0:1], None, ALU.mult),
                reads=["XQHR", "FLG"], writes=["XPC"])
        sch.emit("dve", lambda v: v.tensor_copy(XPC16[:, :], XPC[:, :]), reads=["XPC"], writes=["XPC16"])
        for c in range(NCH):
            wr = [f"DXN{c}", f"XR{c}"] + XR_ALIAS[c]
            sch.emit("dve", lambda v, c=c: v.tensor_sub(
                XR16[:, c * T + 1:c * T + T], XN16[:, c * T:c * T + T - 1], XN16[:, c * T + 1:c * T + T]),
                reads=[f"XN{c}"], writes=wr)
            sch.emit("dve", lambda v, c=c: v.tensor_sub(
                XR16[:, c * T:c * T + 1], XPC16[:, c:c + 1],
                XN16[:, c * T:c * T + 1]),
                reads=["XPC16", f"XN{c}"], writes=[f"DXN{c}"])

        if S2_STOP == 2:
            return
        for sc_i in range(NSC):
            t0 = sc_i * SC
            sch.dma_emit("pool", lambda g: g.dma_start(
                out=MW1S[:, 0:1280].rearrange("p (a b) -> p a b", a=NCH),
                in_=MW1.rearrange("(a p) c -> p a c", p=128)),
                writes=["WSL0", "WSL1", "MW1S"], extra=[("c", "pe", sch.cnt["pe"])])
            for c in range(NCH):
                sch.emit("dve", lambda v, c=c, t0=t0: v.scalar_tensor_tensor(
                    XIB[:, c * SC:(c + 1) * SC], XR16[:, c * T + t0:c * T + t0 + SC],
                    TMAS[:, c:c + 1], XN16[:, c * T + t0:c * T + t0 + SC],
                    ALU.mult, ALU.add),
                    reads=[f"DXN{c}", f"XN{c}", "TMAS"], writes=["XIB", "OGTM"])
            for c in range(NCH):
                sch.emit("pe", lambda t, c=c: t.matmul(
                    PS[6][:, 0:SC], MW1S[:, c * 160:c * 160 + 128], XIB[:, c * SC:(c + 1) * SC],
                    start=(c == 0), stop=(c == NCH - 1)),
                    reads=["XIB", "MW1S"], writes=["ps6"] if c in (0, NCH - 1) else [])
            for c in range(NCH):
                sch.emit("pe", lambda t, c=c: t.matmul(
                    PS[7][0:32, 0:SC], MW1S[:, c * 160 + 128:c * 160 + 160], XIB[:, c * SC:(c + 1) * SC],
                    start=(c == 0), stop=(c == NCH - 1)),
                    reads=["XIB", "MW1S"], writes=["ps7"] if c in (0, NCH - 1) else [])
            for fam, (tn, base) in enumerate([(TNH1, 0), (TNH1, 64), (TNH2, 0), (TNH2, 64), (TNH3, 0)]):
                src = PS[6][fam * 32:(fam + 1) * 32, 0:SC] if fam < 4 else PS[7][0:32, 0:SC]
                sch.emit("act", lambda a, tn=tn, base=base, src=src: a.activation(
                    tn[base:base + 32, 0:SC], src, AF.Tanh),
                    reads=["ps6" if fam < 4 else "ps7"], writes=[f"TNHD{fam}"])
            if S2_STOP == 3:
                break
            for i in range(5):
                tnh_t, tnh_b = [(TNH1, 0), (TNH1, 64), (TNH2, 0), (TNH2, 64), (TNH3, 0)][i]
                sch.dma_emit("pool", lambda g, i=i, tnh_b=tnh_b: g.dma_start(
                    out=MW2B[tnh_b:tnh_b + 33, :], in_=MW2X[i * 33:(i + 1) * 33, :]),
                    writes=[f"MW2_{i}", "DW2R"], extra=[("c", "pe", sch.cnt["pe"])])
                for oc in range(NCH):
                    ph = psum_half(oc)
                    sch.emit("pe", lambda t, i=i, oc=oc, ph=ph,
                             tnh_t=tnh_t, tnh_b=tnh_b, t0=t0: t.matmul(
                        ph, MW2B[tnh_b:tnh_b + 33, oc * 128:(oc + 1) * 128],
                        tnh_t[tnh_b:tnh_b + 33, 0:SC],
                        start=True, stop=True,
                        tile_position=(tnh_b, 0)),
                        reads=[f"MW2_{i}", f"TNHD{i}"], writes=[f"ps{oc // 2}"])
                    eng2 = "pool" if oc % 2 == 0 else "dve"
                    sch.emit("dve", lambda v, oc=oc, ph=ph, t0=t0: v.tensor_tensor(
                        SCB[:, 0:SC], XR16[:, oc * T + t0:oc * T + t0 + SC], ph, ALU.mult),
                        reads=[f"ps{oc // 2}", f"DXN{oc}"], writes=["SCB"])
                    sch.emit(eng2, lambda v, oc=oc, t0=t0: v.tensor_tensor(
                        XIB[:, oc * SC:(oc + 1) * SC], SCB[:, 0:SC],
                        XN16[:, oc * T + t0:oc * T + t0 + SC], ALU.add),
                        reads=["SCB", f"XN{oc}"], writes=["XIB", "OGTM"])
                if DEBUG and sc_i == 0 and i == 0:
                    DXI = nc.dram_tensor("DXI", [128, 2048], BF16, kind="ExternalOutput").ap()
                    sch.dma_emit("sp", lambda s: s.dma_start(out=DXI, in_=XIB[:, :]), reads=["XIB"])
                    DTN1 = nc.dram_tensor("DTN1", [128, 256], BF16, kind="ExternalOutput").ap()
                    DTN2 = nc.dram_tensor("DTN2", [128, 256], BF16, kind="ExternalOutput").ap()
                    DXR0 = nc.dram_tensor("DXR0", [128, 256], BF16, kind="ExternalOutput").ap()
                    DXN0 = nc.dram_tensor("DXN0", [128, 256], BF16, kind="ExternalOutput").ap()
                    sch.dma_emit("sp", lambda s: s.dma_start(out=DTN1, in_=TNH1[:, :]), reads=["TNHD0", "TNHD1"])
                    sch.dma_emit("sp", lambda s: s.dma_start(out=DTN2, in_=TNH2[:, :]), reads=["TNHD2", "TNHD3"])
                    sch.dma_emit("sp", lambda s: s.dma_start(out=DXR0, in_=XR16[:, 0:256]), reads=["DXN0"])
                    sch.dma_emit("sp", lambda s: s.dma_start(out=DXN0, in_=XN16[:, 0:256]), reads=["XN0"])
                if i in (0, 1):
                    dstbuf = Q16 if i == 0 else K16
                    dkey = "Q16" if i == 0 else "K16"
                    for f in range(NCH):
                        wsl = WSL[:, (f % 2) * 1024:(f % 2) * 1024 + 1024]
                        sch.dma_emit("pool", lambda s, i=i, f=f, wsl=wsl: s.dma_start(
                            out=wsl, in_=WQKV[i * D + f * 128:i * D + (f + 1) * 128, :]),
                            writes=[f"WSL{f % 2}"], extra=[("c", "pe", sch.cnt["pe"])])
                        for oc in range(NCH):
                            sch.emit("pe", lambda t, f=f, oc=oc, wsl=wsl: t.matmul(
                                PS[oc][:, 0:SC], wsl[:, oc * 128:(oc + 1) * 128],
                                XIB[:, f * SC:(f + 1) * SC],
                                start=(f == 0), stop=(f == NCH - 1)),
                                reads=["XIB", f"WSL{f % 2}"],
                                writes=[f"ps{oc}"] if f in (0, NCH - 1) else [])
                    for oc in range(NCH):
                        wr = [dkey] + ([f"RSg{g}" for g in range(4)] if oc == 0 else [])
                        sch.emit("act", lambda a, oc=oc, dstbuf=dstbuf: a.activation(
                            dstbuf[:, oc * SC:(oc + 1) * SC], PS[oc][:, 0:SC], AF.Identity),
                            reads=[f"ps{oc}"], writes=wr)
                elif i in (2, 4):
                    for f in range(NCH):
                        wsl = WSL[:, (f % 2) * 1024:(f % 2) * 1024 + 1024]
                        sch.dma_emit("pool", lambda s, i=i, f=f, wsl=wsl: s.dma_start(
                            out=wsl, in_=WQKV[(2 if i == 2 else 3) * D + f * 128:
                                              (2 if i == 2 else 3) * D + (f + 1) * 128, :]),
                            writes=[f"WSL{f % 2}"], extra=[("c", "pe", sch.cnt["pe"])])
                        for tt in range(2):
                            for vh in range(2):
                                pb = PS[4 + 2 * tt + vh][:, 0:512]
                                sch.emit("pe", lambda t, f=f, tt=tt, vh=vh, pb=pb, wsl=wsl: t.matmul(
                                    pb, XIB[:, f * SC + tt * 128:f * SC + tt * 128 + 128],
                                    wsl[:, vh * 512:(vh + 1) * 512],
                                    start=(f == 0), stop=(f == NCH - 1)),
                                    reads=["XIB", f"WSL{f % 2}"],
                                    writes=[f"ps{4 + 2 * tt + vh}"] if f in (0, NCH - 1) else [])
                    for tt in range(2):
                        for vh in range(2):
                            pb = PS[4 + 2 * tt + vh][:, 0:512]
                            pk = f"ps{4 + 2 * tt + vh}"
                            if i == 2:
                                sch.emit("dve", lambda v, tt=tt, vh=vh, pb=pb: v.tensor_copy(
                                    VTMB[:, tt * 1024 + vh * 512:tt * 1024 + (vh + 1) * 512], pb),
                                    reads=[pk], writes=["VTMB", "W2SL0", "W2SL1"])
                            else:
                                sch.emit("dve", lambda v, tt=tt, vh=vh, pb=pb: v.tensor_tensor(
                                    SCB[:, 0:512], pb, BGREP[:, vh * 512:(vh + 1) * 512], ALU.add),
                                    reads=[pk, "BGREP"], writes=["SCB"])
                                sch.emit("act", lambda a, tt=tt, vh=vh: a.activation(
                                    GTMB[:, tt * 1024 + vh * 512:tt * 1024 + (vh + 1) * 512],
                                    SCB[:, 0:512], AF.Silu),
                                    reads=["SCB"], writes=["GTMB", "OGFM"])
                    if i == 4 and S2_FP8:
                        sch.dma_emit("act", lambda a, sc_i=sc_i: a.dma_start(
                            out=GAD.ap()[:, sc_i * 2048:(sc_i + 1) * 2048], in_=GTMB[:, :]),
                            reads=["GTMB"], writes=["GAD"])
                else:
                    for f in range(NCH):
                        sch.emit("pe", lambda t, f=f: t.matmul(
                            PS[4][0:64, 0:SC], DW1S[:, f * 64:(f + 1) * 64],
                            XIB[:, f * SC:(f + 1) * SC],
                            start=(f == 0), stop=(f == NCH - 1)),
                            reads=["XIB", "DW1S"], writes=["ps4"] if f in (0, NCH - 1) else [])
                    sch.emit("act", lambda a: a.activation(TH64[:, 0:SC], PS[4][0:64, 0:SC], AF.Tanh),
                             reads=["ps4"], writes=["TH64"])
                    sch.dma_emit("pool", lambda g: g.dma_start(out=DW2R[0:64, :], in_=DW2),
                                 writes=["DW2R", "MW2_3x"], extra=[("c", "pe", sch.cnt["pe"])])
                    for oc in range(NCH):
                        sch.emit("pe", lambda t, oc=oc: t.matmul(
                            psum_half(oc), DW2R[0:64, oc * 128:(oc + 1) * 128], TH64[:, 0:SC],
                            start=True, stop=True),
                            reads=["TH64", "DW2R"], writes=[f"ps{oc // 2}"])
                        sch.emit("dve", lambda v, oc=oc: v.tensor_scalar(
                            DECB[:, oc * SC:(oc + 1) * SC], psum_half(oc), TDS[:, oc:oc + 1], None, ALU.add),
                            reads=[f"ps{oc // 2}", "TDS"], writes=["DECB"] + ([f"TMPR{g}" for g in range(4)] if oc == 0 else []))
            if S2_STOP == 4:
                break
            if DEBUG and sc_i == 0:
                DDEC = nc.dram_tensor("DDEC", [128, 2048], BF16, kind="ExternalOutput").ap()
                DQ16 = nc.dram_tensor("DQ16", [128, 2048], BF16, kind="ExternalOutput").ap()
                DK16 = nc.dram_tensor("DK16", [128, 2048], BF16, kind="ExternalOutput").ap()
                DVTM = nc.dram_tensor("DVTM", [128, 2048], BF16, kind="ExternalOutput").ap()
                DGTM = nc.dram_tensor("DGTM", [128, 2048], BF16, kind="ExternalOutput").ap()
                sch.dma_emit("sp", lambda s: s.dma_start(out=DDEC, in_=DECB[:, :]), reads=["DECB"])
                sch.dma_emit("sp", lambda s: s.dma_start(out=DQ16, in_=Q16[:, :]), reads=["Q16"])
                sch.dma_emit("sp", lambda s: s.dma_start(out=DK16, in_=K16[:, :]), reads=["K16"])
                sch.dma_emit("sp", lambda s: s.dma_start(out=DVTM, in_=VTMB[:, :]), reads=["VTMB"])
                sch.dma_emit("sp", lambda s: s.dma_start(out=DGTM, in_=GTMB[:, :]), reads=["GTMB"])
            sch.serialize = True
            for hp in range(NCH):
                sch.emit("act", lambda a, hp=hp: a.activation(EXPG[:, 0:SC], DECB[:, hp * SC:(hp + 1) * SC], AF.Exp),
                         reads=["DECB"], writes=["EXPG"])
                sch.emit("dve", lambda v: v.memset(GAM[:, 0:1], 0.0), writes=["GAM"])
                sch.emit("dve", lambda v: v.tensor_tensor_scan(
                    GAM[:, 1:SC + 1], ONESW[:, 0:SC],
                    EXPG[:, 0:SC], 0.0, ALU.mult, ALU.subtract),
                    reads=["EXPG", "ONESW"], writes=["GAM"])
                if DEBUG and sc_i == 0 and hp == 0:
                    DGAM = nc.dram_tensor("DGAM", [128, 264], F32, kind="ExternalOutput").ap()
                    sch.dma_emit("sp", lambda s: s.dma_start(out=DGAM, in_=GAM[:, :]), reads=["GAM"])
                sch.emit("act", lambda a: a.activation(SCB[:, 0:SC], EXPG[:, 0:SC], AF.Exp, scale=-1.0),
                         reads=["EXPG"], writes=["SCB"])
                sch.emit("dve", lambda v: v.tensor_scalar(SCB[:, 0:SC], SCB[:, 0:SC], -1.0, 1.0, ALU.mult, ALU.add),
                         reads=["SCB"], writes=["SCB"])
                sch.emit("dve", lambda v, hp=hp: v.tensor_tensor(
                    K16[:, hp * SC:(hp + 1) * SC], K16[:, hp * SC:(hp + 1) * SC], SCB[:, 0:SC], ALU.mult),
                    reads=["K16", "SCB"], writes=["K16"])
                for cc in range(2) if S2_SCAN else []:
                    c0 = cc * 128
                    sch.barrier()
                    sch.emit("dve", lambda v, c0=c0: v.tensor_scalar(
                        DLT[:, 0:128], GAM[:, 1 + c0:1 + c0 + 128], GAM[:, c0:c0 + 1], None, ALU.subtract),
                        reads=["GAM"], writes=["DLT", "STAT"])
                    sch.emit("act", lambda a: a.activation(EQG[:, :], DLT[:, 0:128], AF.Exp),
                             reads=["DLT"], writes=["EQG"])
                    sch.emit("dve", lambda v: v.tensor_scalar(
                        EPS_T[:, 0:64], DLT[:, 0:64], DLT[:, 0:1], None, ALU.subtract),
                        reads=["DLT"], writes=["EPS", "FARG", "K2A", "STAT"])
                    sch.emit("dve", lambda v: v.tensor_scalar(
                        EPS_T[:, 64:128], DLT[:, 64:128], DLT[:, 64:65], None, ALU.subtract),
                        reads=["DLT"], writes=["EPS", "FARG", "K2A", "STAT"])
                    sch.emit("dve", lambda v, hp=hp, cc=cc, c0=c0: v.scalar_tensor_tensor(
                        QG[:, :], Q16[:, hp * SC + c0:hp * SC + c0 + 128], 0.125, EQG[:, :],
                        ALU.mult, ALU.mult),
                        reads=["Q16", "EQG"], writes=["QG"])
                    if DEBUG and sc_i == 0 and hp == 0 and cc == 0:
                        DQGD = nc.dram_tensor("DQGD", [128, 128], BF16, kind="ExternalOutput").ap()
                        DEQG = nc.dram_tensor("DEQG", [128, 128], F32, kind="ExternalOutput").ap()
                        sch.dma_emit("sp", lambda s: s.dma_start(out=DQGD, in_=QG[:, :]), reads=["QG"])
                        sch.dma_emit("sp", lambda s: s.dma_start(out=DEQG, in_=EQG[:, :]), reads=["EQG"])
                    sch.emit("act", lambda a: a.activation(SCR1[:, 0:128], EPS_T[:, 0:128], AF.Exp),
                             reads=["EPS"], writes=["ELQ", "SCR1", "STAT"])
                    sch.emit("dve", lambda v, hp=hp, cc=cc, c0=c0: v.scalar_tensor_tensor(
                        QL[:, :], Q16[:, hp * SC + c0:hp * SC + c0 + 128], 0.125, SCR1[:, 0:128],
                        ALU.mult, ALU.mult),
                        reads=["Q16", "ELQ"], writes=["QL"])
                    if DEBUG and sc_i == 0 and hp == 0 and cc == 0:
                        DQLD = nc.dram_tensor("DQLD", [128, 128], BF16, kind="ExternalOutput").ap()
                        sch.dma_emit("sp", lambda s: s.dma_start(out=DQLD, in_=QL[:, :]), reads=["QL"])
                    sch.emit("dve", lambda v: v.tensor_scalar(
                        SCR1[:, 0:128], EPS_T[:, 0:128], -1.0, 80.0, ALU.mult, ALU.min),
                        reads=["EPS"], writes=["ELK0", "SCR1", "STAT"])
                    sch.emit("act", lambda a: a.activation(SCR1[:, 0:128], SCR1[:, 0:128], AF.Exp),
                             reads=["ELK0"], writes=["ELK", "SCR1"])
                    sch.emit("dve", lambda v, hp=hp, cc=cc, c0=c0: v.tensor_tensor(
                        KL[:, :], K16[:, hp * SC + c0:hp * SC + c0 + 128], SCR1[:, 0:128], ALU.mult),
                        reads=["K16", "ELK"], writes=["KL"])
                    if DEBUG and sc_i == 0 and hp == 0 and cc == 0:
                        DKLD = nc.dram_tensor("DKLD", [128, 128], BF16, kind="ExternalOutput").ap()
                        sch.dma_emit("sp", lambda s: s.dma_start(out=DKLD, in_=KL[:, :]), reads=["KL"])
                    sch.emit("dve", lambda v: v.tensor_scalar(
                        EPS_T[:, 0:128], DLT[:, 0:128], DLT[:, 64:65], None, ALU.subtract),
                        reads=["DLT"], writes=["EPS", "FARG", "K2A", "STAT"])
                    sch.emit("act", lambda a: a.activation(SCR1[:, 0:128], EPS_T[:, 64:128] if False else EPS_T[:, 0:128], AF.Exp),
                             reads=["FARG"], writes=["EFQ", "SCR1", "STAT"])
                    sch.emit("dve", lambda v, hp=hp, cc=cc, c0=c0: v.scalar_tensor_tensor(
                        QF[:, 0:64], Q16[:, hp * SC + c0 + 64:hp * SC + c0 + 128], 0.125, SCR1[:, 64:128],
                        ALU.mult, ALU.mult),
                        reads=["Q16", "EFQ"], writes=["QF"])
                    sch.emit("dve", lambda v: v.tensor_scalar(
                        SCR1[:, 0:64], EPS_T[:, 0:64], -1.0, 80.0, ALU.mult, ALU.min),
                        reads=["FARG"], writes=["EGF0", "SCR1", "STAT"])
                    sch.emit("act", lambda a: a.activation(SCR1[:, 0:64], SCR1[:, 0:64], AF.Exp),
                             reads=["EGF0"], writes=["EGF", "SCR1"])
                    sch.emit("dve", lambda v, hp=hp, cc=cc, c0=c0: v.tensor_tensor(
                        KF[:, 0:64], K16[:, hp * SC + c0:hp * SC + c0 + 64], SCR1[:, 0:64], ALU.mult),
                        reads=["K16", "EGF"], writes=["KF"])
                    sch.emit("dve", lambda v: v.tensor_scalar(
                        EPS_T[:, 0:128], DLT[:, 0:128], DLT[:, 127:128], None, ALU.subtract),
                        reads=["DLT"], writes=["EPS", "FARG", "K2A", "STAT"])
                    sch.emit("act", lambda a: a.activation(SCR1[:, 0:128], EPS_T[:, 0:128], AF.Exp, scale=-1.0),
                             reads=["K2A"], writes=["EK2", "SCR1", "STAT"])
                    sch.emit("dve", lambda v, hp=hp, cc=cc, c0=c0: v.tensor_tensor(
                        K2[:, :], K16[:, hp * SC + c0:hp * SC + c0 + 128], SCR1[:, 0:128], ALU.mult),
                        reads=["K16", "EK2"], writes=["K2"])
                    if S2_FP8:
                        sch.emit("act", lambda a, hp=hp: a.activation(EGC[:, hp:hp + 1], GCOFF[:, hp:hp + 1], AF.Exp),
                                 reads=["GCOFF"], writes=["EGC"])
                        sch.emit("dve", lambda v, hp=hp, cc=cc, sc_i=sc_i: v.tensor_scalar(
                            QC16[:, sc_i * 2048 + hp * 256 + cc * 128: sc_i * 2048 + hp * 256 + cc * 128 + 128],
                            QG[:, :], EGC[:, hp:hp + 1], None, ALU.mult),
                            reads=["QG", "EGC"], writes=["QC16"] + [f"W{w}" for w in range(8)] + ["KS0", "KS1"])
                    sch.emit("dve", lambda v, hp=hp: v.tensor_tensor(
                        GCOFF[:, hp:hp + 1], GCOFF[:, hp:hp + 1], DLT[:, 127:128], ALU.add),
                        reads=["GCOFF", "DLT"], writes=["GCOFF"])
                    sch.emit("pe", lambda t: t.matmul(
                        PS[4][0:64, 0:64], KL[0:64, 0:64], QL[0:64, 0:64],
                        start=True, stop=True),
                        reads=["KL", "QL"], writes=["ps4"])
                    sch.emit("pe", lambda t: t.matmul(
                        PS[4][64:128, 64:128], KL[0:64, 64:128], QL[0:64, 64:128],
                        start=True, stop=True, tile_position=(0, 64)),
                        reads=["KL", "QL"], writes=["ps4"])
                    sch.emit("pe", lambda t: t.matmul(
                        PS[4][0:64, 64:128], KF[0:64, 0:64], QF[0:64, 0:64],
                        start=True, stop=True),
                        reads=["KF", "QF"], writes=["ps4"])
                    sch.emit("dve", lambda v: v.memset(KL[0:64, :], 0.0),
                             reads=[], writes=["KL"])
                    sch.emit("dve", lambda v: v.memset(KF[0:64, 0:64], 0.0),
                             reads=[], writes=["KF"])
                    sch.emit("pe", lambda t: t.matmul(
                        PS[4][0:64, 128:192], KL[:, 0:64], QL[:, 0:64],
                        start=True, stop=True),
                        reads=["KL", "QL"], writes=["ps4"])
                    sch.emit("pe", lambda t: t.matmul(
                        PS[4][64:128, 192:256], KL[:, 64:128], QL[:, 64:128],
                        start=True, stop=True, tile_position=(0, 64)),
                        reads=["KL", "QL"], writes=["ps4"])
                    sch.emit("pe", lambda t: t.matmul(
                        PS[4][0:64, 192:256], KF[:, 0:64], QF[:, 0:64],
                        start=True, stop=True),
                        reads=["KF", "QF"], writes=["ps4"])
                    for h in range(2):
                        sch.emit("dve", lambda v, h=h: v.tensor_tensor(
                            ASB[:, h * 128:(h + 1) * 128], PS[4][:, h * 128:(h + 1) * 128], CAUS[:, :], ALU.mult),
                            reads=["ps4", "CAUS"], writes=["ASB"])
                    if DEBUG and sc_i == 0 and hp == 0 and cc == 0:
                        DASB = nc.dram_tensor("DASB", [128, 256], BF16, kind="ExternalOutput").ap()
                        sch.dma_emit("sp", lambda s: s.dma_start(out=DASB, in_=ASB[:, :]), reads=["ASB"])
                    if S2_LVL < 3:
                        continue
                    sch.emit("pe", lambda t: t.transpose(
                        PS[7].bitcast(BF16)[:, 0:128], K2[:, :], IDEN[:, :]),
                        reads=["K2", "IDEN"], writes=["ps7"])
                    sch.emit("dve", lambda v: v.tensor_copy(K2T[:, :], PS[7].bitcast(BF16)[:, 0:128]),
                             reads=["ps7"], writes=["K2T"])
                    if S2_LVL < 4:
                        continue
                    sch.emit("dve", lambda v, hp=hp: v.tensor_copy(
                        S16D[0:64, 0:64], SST[0:64, hp * 64:(hp + 1) * 64]),
                        reads=["SST"], writes=["S16D"])
                    sch.emit("dve", lambda v, hp=hp: v.tensor_copy(
                        S16D[64:128, 64:128], SST[64:128, hp * 64:(hp + 1) * 64]),
                        reads=["SST"], writes=["S16D"])
                    for h in range(2):
                        sch.emit("pe", lambda t, h=h, hp=hp, cc=cc: t.matmul(
                            PS[5][:, h * 64:(h + 1) * 64], ASB[:, h * 128:(h + 1) * 128],
                            VTMB[:, cc * 1024 + hp * 128 + h * 64: cc * 1024 + hp * 128 + (h + 1) * 64],
                            start=True, stop=False),
                            reads=["ASB", "VTMB"], writes=["ps5"])
                        sch.emit("pe", lambda t, h=h: t.matmul(
                            PS[5][:, h * 64:(h + 1) * 64], QG[:, :], S16D[:, h * 64:(h + 1) * 64],
                            start=False, stop=True),
                            reads=["QG", "S16D"], writes=["ps5"])
                    if DEBUG and sc_i == 0 and hp == 0 and cc == 0:
                        DPS5 = nc.dram_tensor("DPS5", [128, 128], F32, kind="ExternalOutput").ap()
                        PS5C = SCB[:, 256:512].bitcast(F32)
                        sch.emit("dve", lambda v, PS5C=PS5C: v.tensor_copy(PS5C, PS[5][:, 0:128]),
                                 reads=["ps5"], writes=["PS5C"])
                        sch.dma_emit("sp", lambda s, DPS5=DPS5, PS5C=PS5C: s.dma_start(
                            out=DPS5, in_=PS5C), reads=["PS5C"])
                    sch.emit("dve", lambda v, hp=hp, cc=cc, c0=c0: v.tensor_tensor(
                        OGTM[:, cc * 1024 + hp * 128: cc * 1024 + (hp + 1) * 128], PS[5][:, 0:128],
                        GTMB[:, cc * 1024 + hp * 128: cc * 1024 + (hp + 1) * 128], ALU.mult),
                        reads=["ps5", "GTMB"], writes=["OGTM", "XIB"])
                    for h in range(2):
                        hb = h * 64
                        sch.emit("pe", lambda t, h=h, hb=hb, hp=hp, cc=cc: t.matmul(
                            PS[6][hb:hb + 64, 0:64], K2T[:, hb:hb + 64],
                            VTMB[:, cc * 1024 + hp * 128 + hb: cc * 1024 + hp * 128 + hb + 64],
                            start=True, stop=True, tile_position=(0, hb)),
                            reads=["K2T", "VTMB"], writes=["ps6"])
                    sch.emit("dve", lambda v, hp=hp: v.tensor_scalar(
                        SUPD[:, 0:64], SST[:, hp * 64:(hp + 1) * 64], EQG[:, 127:128], None, ALU.mult),
                        reads=["SST", "EQG"], writes=["SUPD", "STAT"])
                    sch.emit("dve", lambda v, hp=hp: v.tensor_tensor(
                        SST[:, hp * 64:(hp + 1) * 64], SUPD[:, 0:64], PS[6][:, 0:64], ALU.add),
                        reads=["SUPD", "ps6"], writes=["SST"])
            if S2_STOP == 5:
                break
            if DEBUG and sc_i == 0:
                DSST0 = nc.dram_tensor("DSST0", [128, 512], F32, kind="ExternalOutput").ap()
                DOG = nc.dram_tensor("DOG", [128, 2048], BF16, kind="ExternalOutput").ap()
                sch.dma_emit("sp", lambda s: s.dma_start(out=DSST0, in_=SST[:, :]), reads=["SST"])
                sch.dma_emit("sp", lambda s: s.dma_start(out=DOG, in_=OGTM[:, :]), reads=["OGTM"])
            for vc in range(NCH):
                for tt in range(2):
                    sch.emit("pe", lambda t, vc=vc, tt=tt: t.transpose(
                        PS[7].bitcast(BF16)[:, 0:128], OGTM[:, tt * 1024 + vc * 128:tt * 1024 + (vc + 1) * 128],
                        IDEN[:, :]),
                        reads=["OGTM", "IDEN"], writes=["ps7"])
                    sch.emit("dve", lambda v, vc=vc, tt=tt: v.tensor_copy(
                        OGFM[:, vc * SC + tt * 128:vc * SC + tt * 128 + 128],
                        PS[7].bitcast(BF16)[:, 0:128]),
                        reads=["ps7"], writes=["OGFM", "GTMB"])
            if DEBUG and sc_i == 0:
                DOFM = nc.dram_tensor("DOFM", [128, 2048], BF16, kind="ExternalOutput").ap()
                sch.dma_emit("sp", lambda s: s.dma_start(out=DOFM, in_=OGFM[:, :]), reads=["OGFM"])
            for f in range(NCH):
                wsl = WSL[:, (f % 2) * 1024:(f % 2) * 1024 + 1024]
                sch.dma_emit("pool", lambda s, f=f, wsl=wsl: s.dma_start(
                    out=wsl, in_=WQKV[4 * D + f * 128:4 * D + (f + 1) * 128, :]),
                    writes=[f"WSL{f % 2}"], extra=[("c", "pe", sch.cnt["pe"])])
                for oc in range(NCH):
                    sch.emit("pe", lambda t, f=f, oc=oc, wsl=wsl: t.matmul(
                        PS[oc][:, 0:SC], wsl[:, oc * 128:(oc + 1) * 128], OGFM[:, f * SC:(f + 1) * SC],
                        start=(f == 0), stop=(f == NCH - 1)),
                        reads=["OGFM", f"WSL{f % 2}"],
                        writes=[f"ps{oc}"] if f in (0, NCH - 1) else [])
            for oc in range(NCH):
                sch.emit("dve", lambda v, oc=oc, t0=t0: v.tensor_tensor(
                    X[:, oc * T + t0:oc * T + t0 + SC], X[:, oc * T + t0:oc * T + t0 + SC],
                    PS[oc][:, 0:SC], ALU.add),
                    reads=[f"ps{oc}", f"X{oc}"], writes=[f"X{oc}"])
        if S2_STOP:
            return
        if not S2_CORR:
            return
        sch.dma_emit("pool", lambda g: g.dma_start(out=CC2_IN.ap(), in_=SST[:, :]),
                     reads=["SST"], writes=["C2I"])
        sch.cc_emit(lambda g: g.collective_compute(
            "AllGather", ALU.bypass, replica_groups=[[0, 1], [2, 3], [4, 5], [6, 7]],
            ins=[CC2_IN.ap()], outs=[CC2_OUT.ap()]),
            reads=["C2I"], writes=["C2O"])
        sch.dma_emit("pool", lambda g: g.dma_start(out=SRC16[:, :], in_=CC2_OUT.ap()[0:128, :]),
                     reads=["C2O"], writes=["SRC16", "DECB"])
        sch.emit("dve", lambda v: v.tensor_scalar(SRC16[:, :], SRC16[:, :], FLG[:, 0:1], None, ALU.mult),
                 reads=["SRC16", "FLG"], writes=["SRC16"])
        SDB = MW2B
        sch.emit("dve", lambda v: v.memset(SDB[:, :], 0.0),
                 writes=["SDB", "MW2_0", "MW2_1", "MW2_2", "MW2_3", "MW2_4", "DW2R", "MW2_3x"])
        for hp in range(NCH):
            sch.emit("dve", lambda v, hp=hp: v.tensor_copy(
                SDB[0:64, hp * 128:hp * 128 + 64], SRC16[0:64, hp * 64:(hp + 1) * 64]),
                reads=["SRC16"], writes=["SDB"])
            sch.emit("dve", lambda v, hp=hp: v.tensor_copy(
                SDB[64:128, hp * 128 + 64:hp * 128 + 128], SRC16[64:128, hp * 64:(hp + 1) * 64]),
                reads=["SRC16"], writes=["SDB"])
        for sc_i in range(NSC):
            t0 = sc_i * SC
            sch.dma_emit("act", lambda a, sc_i=sc_i: a.dma_start(
                out=K16[:, :], in_=GAD.ap()[:, sc_i * 2048:(sc_i + 1) * 2048]),
                reads=["GAD"], writes=["K16"])
            for cc in range(2):
                for hp in range(NCH):
                    base = sc_i * 2048 + hp * 256 + cc * 128
                    sch.emit("pe", lambda t, hp=hp, base=base: t.matmul(
                        PS[5][:, 0:128], QC16[:, base:base + 128],
                        SDB[:, hp * 128:(hp + 1) * 128],
                        start=True, stop=True),
                        reads=["QC16", "SDB"], writes=["ps5"])
                    sch.emit("dve", lambda v, hp=hp, cc=cc: v.tensor_tensor(
                        OGTM[:, cc * 1024 + hp * 128: cc * 1024 + (hp + 1) * 128], PS[5][:, 0:128],
                        K16[:, cc * 1024 + hp * 128: cc * 1024 + (hp + 1) * 128],
                        ALU.mult),
                        reads=["ps5", "K16"], writes=["OGTM"])
            for vc in range(NCH):
                for tt in range(2):
                    sch.emit("pe", lambda t, vc=vc, tt=tt: t.transpose(
                        PS[7].bitcast(BF16)[:, 0:128], OGTM[:, tt * 1024 + vc * 128:tt * 1024 + (vc + 1) * 128],
                        IDEN[:, :]),
                        reads=["OGTM", "IDEN"], writes=["ps7"])
                    sch.emit("dve", lambda v, vc=vc, tt=tt: v.tensor_copy(
                        OGFM[:, vc * SC + tt * 128:vc * SC + tt * 128 + 128],
                        PS[7].bitcast(BF16)[:, 0:128]),
                        reads=["ps7"], writes=["OGFM", "GTMB"])
            for f in range(NCH):
                wsl = WSL[:, (f % 2) * 1024:(f % 2) * 1024 + 1024]
                sch.dma_emit("pool", lambda s, f=f, wsl=wsl: s.dma_start(
                    out=wsl, in_=WQKV[4 * D + f * 128:4 * D + (f + 1) * 128, :]),
                    writes=[f"WSL{f % 2}"], extra=[("c", "pe", sch.cnt["pe"])])
                for oc in range(NCH):
                    sch.emit("pe", lambda t, f=f, oc=oc, wsl=wsl: t.matmul(
                        PS[oc][:, 0:SC], wsl[:, oc * 128:(oc + 1) * 128], OGFM[:, f * SC:(f + 1) * SC],
                        start=(f == 0), stop=(f == NCH - 1)),
                        reads=["OGFM", f"WSL{f % 2}"],
                        writes=[f"ps{oc}"] if f in (0, NCH - 1) else [])
            for oc in range(NCH):
                sch.emit("dve", lambda v, oc=oc, t0=t0: v.tensor_tensor(
                    X[:, oc * T + t0:oc * T + t0 + SC], X[:, oc * T + t0:oc * T + t0 + SC],
                    PS[oc][:, 0:SC], ALU.add),
                    reads=[f"ps{oc}", f"X{oc}"], writes=[f"X{oc}"])
        sch.emit("dve", lambda v: v.memset(KQ[:, 8:9], 0.0),
                 writes=["QC16"] + [f"W{w}" for w in range(8)] + ["KS0", "KS1"])
        sch.serialize = False

    L1A = din("L1A", [64, 64])
    L1AT = din("L1AT", [64, 64])
    L1BM = din("L1BM", [D, 64])
    L1WIT = din("L1WIT", [64, D])
    L1WRT = din("L1WRT", [D, 64])
    L1CMP = din("L1CMP", [64 * D, 128])
    L1KT = din("L1KT", [128, 32768])
    L1V = din("L1V", [32768, D])
    IOTA1K = din("IOTA1K", [128, 1024])
    P256 = din("P256", [128, 1])
    KTB = nc.dram_tensor("KTB", [128, 32768], BF16)
    SCD = nc.dram_tensor("SCD", [32768, 128], BF16)

    HQBF = HQB[:, :].bitcast(F32)
    HQ_KEYS = [f"HQ{ch}" for ch in range(32)] + [f"W{w}" for w in range(8)] + \
              ["KS0", "KS1", "QC16"]
    XR_KEYS = [f"XR{c}" for c in range(NCH)] + ["M1", "KSTG2", "KSTG2b", "SREP",
                                               "SREPb", "CREP", "CREPb", "ABSB"]
    XN_KEYS = [f"XN{c}" for c in range(NCH)]

    def l1attn():
        ln()
        sch.serialize = True
        UB = HQBF[0:64, 0:2048]
        for cf in range(NCH):
            sch.dma_emit("pool", lambda s, cf=cf: s.dma_start(
                out=WSL[:, 0:64], in_=L1BM[cf * 128:(cf + 1) * 128, :]),
                writes=["WSL0", "WSL1"])
            for g in range(4):
                sch.emit("pe", lambda t, cf=cf, g=g: t.matmul(
                    PS[g][0:64, 0:512], WSL[:, 0:64],
                    XN16[:, cf * T + g * 512: cf * T + (g + 1) * 512],
                    start=(cf == 0), stop=(cf == NCH - 1)),
                    reads=[f"XN{cf}", "WSL0"],
                    writes=[f"ps{g}"] if cf in (0, NCH - 1) else [])
        for g in range(4):
            sch.emit("dve", lambda v, g=g: v.tensor_copy(
                UB[:, g * 512:(g + 1) * 512], PS[g][0:64, 0:512]),
                reads=[f"ps{g}"], writes=["UB"] + (HQ_KEYS if g == 0 else []))
        AS = HQBF[0:64, 2080:2144]
        ATS = HQBF[0:64, 2144:2208]
        LP = [HQBF[0:64, 2208:2272], HQBF[0:64, 2272:2336]]
        RP = [HQBF[0:64, 2336:2400], HQBF[0:64, 2400:2464]]
        A64 = HQBF[0:64, 2464:2528]
        A64T = HQBF[0:64, 2528:2592]
        PP = [HQBF[0:64, 2592:2656], HQBF[0:64, 2656:2720]]
        HS = HQBF[0:64, 2720:2752]
        HH0 = HQBF[0:64, 2752:2753]
        HH1 = HQBF[0:64, 2753:2754]
        HF = HQBF[0:64, 2754:2755]
        WITS = HQBF[0:64, 3072:4096]
        sch.dma_emit("sp", lambda s: s.dma_start(out=AS, in_=L1A), writes=["UB"])
        sch.dma_emit("sp", lambda s: s.dma_start(out=ATS, in_=L1AT), writes=["UB"])
        sch.dma_emit("sp", lambda s: s.dma_start(out=WITS, in_=L1WIT), writes=["UB"])
        sch.emit("dve", lambda v: v.tensor_copy(LP[0], AS), reads=["UB"], writes=["UB"])
        sch.emit("dve", lambda v: v.tensor_copy(RP[0], ATS), reads=["UB"], writes=["UB"])
        for k in range(11):
            a, b = k % 2, (k + 1) % 2
            sch.emit("pe", lambda t, a=a: t.matmul(PS[6][0:64, 0:64], RP[a], LP[a],
                                                   start=True, stop=True),
                     reads=["UB"], writes=["ps6"])
            sch.emit("pe", lambda t, a=a: t.matmul(PS[6][0:64, 64:128], LP[a], RP[a],
                                                   start=True, stop=True),
                     reads=["UB"], writes=["ps6"])
            sch.emit("dve", lambda v, b=b: v.tensor_copy(LP[b], PS[6][0:64, 0:64]),
                     reads=["ps6"], writes=["UB"])
            sch.emit("dve", lambda v, b=b: v.tensor_copy(RP[b], PS[6][0:64, 64:128]),
                     reads=["ps6"], writes=["UB"])
            if k == 5:
                sch.emit("dve", lambda v, b=b: v.tensor_copy(A64, LP[b]), reads=["UB"], writes=["UB"])
                sch.emit("dve", lambda v, b=b: v.tensor_copy(A64T, RP[b]), reads=["UB"], writes=["UB"])
        L11 = LP[11 % 2]
        UBr = HQBF[0:64, 0:2048].rearrange("p (c j) -> p j c", j=64)
        sch.emit("dve", lambda v: v.tensor_copy(HS, UBr[:, 0:1, :]), reads=["UB"], writes=["UB"])
        for j in range(1, 64):
            sch.emit("pe", lambda t: t.matmul(PS[7][0:64, 0:32], AS, HS,
                                              start=True, stop=True),
                     reads=["UB"], writes=["ps7"])
            sch.emit("dve", lambda v, j=j: v.tensor_tensor(
                HS, PS[7][0:64, 0:32], UBr[:, j:j + 1, :], ALU.add),
                reads=["ps7", "UB"], writes=["UB"])
        sch.emit("dve", lambda v: v.tensor_copy(PP[0], A64), reads=["UB"], writes=["UB"])
        for k in range(1, 32):
            a, b = (k - 1) % 2, k % 2
            sch.emit("pe", lambda t, a=a, k=k: t.matmul(
                PS[6][0:64, 0:1], PP[a], HS[:, 31 - k:32 - k],
                start=(k == 1), stop=(k == 31)),
                reads=["UB"], writes=["ps6"] if k in (1, 31) else [])
            if k < 31:
                sch.emit("pe", lambda t, a=a: t.matmul(PS[5][0:64, 0:64], A64T, PP[a],
                                                       start=True, stop=True),
                         reads=["UB"], writes=["ps5"])
                sch.emit("dve", lambda v, b=b: v.tensor_copy(PP[b], PS[5][0:64, 0:64]),
                         reads=["ps5"], writes=["UB"])
        sch.emit("dve", lambda v: v.tensor_tensor(HH0, PS[6][0:64, 0:1], HS[:, 31:32], ALU.add),
                 reads=["ps6", "UB"], writes=["UB"])
        sch.emit("dve", lambda v: v.tensor_copy(XQH[:, 0:1], HQBF[:, 2752:2753]),
                 reads=["UB"], writes=["XQH"])
        sch.dma_emit("pool", lambda g: g.dma_start(out=HALO_IN.ap(), in_=XQH[:, :]),
                     reads=["XQH"], writes=["HIN"])
        sch.cc_emit(lambda g: g.collective_compute(
            "AllGather", ALU.bypass, replica_groups=[[0, 1], [2, 3], [4, 5], [6, 7]],
            ins=[HALO_IN.ap()], outs=[HALO_OUT.ap()]),
            reads=["HIN"], writes=["HOUT"])
        sch.dma_emit("pool", lambda g: g.dma_start(out=XQHR[:, 0:20], in_=HALO_OUT.ap()[0:128, :]),
                     reads=["HOUT"], writes=["XQHR"])
        sch.dma_emit("pool", lambda g: g.dma_start(out=XQHR[:, 20:40], in_=HALO_OUT.ap()[128:256, :]),
                     reads=["HOUT"], writes=["XQHR"])
        sch.emit("dve", lambda v: v.tensor_copy(HH0, XQHR[0:64, 0:1]), reads=["XQHR"], writes=["UB"])
        sch.emit("dve", lambda v: v.tensor_copy(HH1, XQHR[0:64, 20:21]), reads=["XQHR"], writes=["UB"])
        sch.emit("pe", lambda t: t.matmul(PS[6][0:64, 0:1], L11, HH0, start=True, stop=True),
                 reads=["UB"], writes=["ps6"])
        sch.emit("dve", lambda v: v.tensor_tensor(HF, PS[6][0:64, 0:1], HH1, ALU.add),
                 reads=["ps6", "UB"], writes=["UB"])
        HPJ16 = KQ2[:, 0:4].bitcast(BF16)
        for cf in range(NCH):
            sch.emit("pe", lambda t, cf=cf: t.matmul(
                PS[6][:, cf:cf + 1], WITS[:, cf * 128:(cf + 1) * 128], HF,
                start=True, stop=True),
                reads=["UB"], writes=["ps6"])
        sch.emit("dve", lambda v: v.tensor_copy(HPJ16[:, :], PS[6][:, 0:8]),
                 reads=["ps6"], writes=["HPJ", "KQ2r"])
        if L1_STOP == 1:
            sch.serialize = False
            return
        ILG = KQ2[:, 8:24]
        for tb in range(16):
            for cf in range(NCH):
                sch.emit("pe", lambda t, tb=tb, cf=cf: t.matmul(
                    PS[4][:, tb:tb + 1], XN16[:, cf * T + tb * 128: cf * T + tb * 128 + 128],
                    HPJ16[:, cf:cf + 1], start=(cf == 0), stop=(cf == NCH - 1)),
                    reads=[f"XN{cf}", "HPJ"],
                    writes=["ps4"] if cf in (0, NCH - 1) else [])
        sch.emit("dve", lambda v: v.tensor_copy(ILG[:, :], PS[4][:, 0:16]),
                 reads=["ps4"], writes=["ILG"])
        sch.emit("dve", lambda v: v.tensor_reduce(STAT[:, 120:121], ILG[:, :], AX.X, ALU.max),
                 reads=["ILG"], writes=["STAT", "DLT", "EPS", "SCR1", "SUPD"])
        pool_reduce_fenced(KMX[0:1, 0:1], STAT[:, 120:121], ALU.max, "STAT", "L1m", KMX[0:1, 6:7])
        sch.emit("dve", lambda v: v.tensor_copy(KMX[0:1, 4:5], KMX[0:1, 0:1]),
                 reads=["L1m"], writes=["L1mz"])
        bcast128(STAT[:, 121:122], KMX[0:1, 0:1], "L1m", "MLB")
        sch.emit("dve", lambda v: v.tensor_scalar(STAT[:, 122:123], STAT[:, 121:122], -1.0, None, ALU.mult),
                 reads=["MLB"], writes=["NEGM"])
        E16 = SCB[:, 48:64]
        sch.emit("act", lambda a: a.activation(E16[:, :], ILG[:, :], AF.Exp, bias=STAT[:, 122:123]),
                 reads=["ILG", "NEGM"], writes=["E16", "SCB"])
        sch.emit("dve", lambda v: v.tensor_reduce(STAT[:, 123:124], E16[:, :], AX.X, ALU.add),
                 reads=["E16"], writes=["ZCOL", "STAT"])
        pool_reduce_fenced(KMX[0:1, 5:6], STAT[:, 123:124], ALU.add, "ZCOL", "L1z", KMX[0:1, 7:8])
        sch.dma_emit("pool", lambda s: s.dma_start(
            out=DW1S[:, :].rearrange("p (a c) -> p a c", a=NCH),
            in_=L1WRT.rearrange("(a p) c -> p a c", p=128)),
            writes=["DW1S"])
        TPB = GTMB[:, 0:1024]
        for tb in range(16):
            for cf in range(NCH):
                sch.emit("pe", lambda t, tb=tb, cf=cf: t.matmul(
                    PS[5][:, 0:64], XN16[:, cf * T + tb * 128: cf * T + tb * 128 + 128],
                    DW1S[:, cf * 64:(cf + 1) * 64], start=(cf == 0), stop=(cf == NCH - 1)),
                    reads=[f"XN{cf}", "DW1S"],
                    writes=["ps5"] if cf in (0, NCH - 1) else [])
            sch.emit("dve", lambda v: v.tensor_reduce(STAT[:, 116:117], PS[5][:, 0:64], AX.X, ALU.max),
                     reads=["ps5"], writes=["TPmx", "STAT"])
            sch.emit("dve", lambda v: v.tensor_scalar(STAT[:, 117:118], STAT[:, 116:117], -1.0, None, ALU.mult),
                     reads=["TPmx"], writes=["TPng"])
            sch.emit("act", lambda a, tb=tb: a.activation(
                TPB[:, tb * 64:(tb + 1) * 64], PS[5][:, 0:64], AF.Exp, bias=STAT[:, 117:118]),
                reads=["ps5", "TPng"], writes=["TPB", "GTMB", "HSx", "OGFM", "LB"])
            sch.emit("dve", lambda v, tb=tb: v.tensor_reduce(
                STAT[:, 118:119], TPB[:, tb * 64:(tb + 1) * 64], AX.X, ALU.add),
                reads=["TPB"], writes=["TPsm", "STAT"])
            sch.emit("act", lambda a: a.activation(STAT[:, 119:120], STAT[:, 118:119], AF.Ln),
                     reads=["TPsm"], writes=["TPln"])
            sch.emit("act", lambda a: a.activation(STAT[:, 119:120], STAT[:, 119:120], AF.Exp, scale=-1.0),
                     reads=["TPln"], writes=["TPiv"])
            sch.emit("dve", lambda v, tb=tb: v.tensor_scalar(
                TPB[:, tb * 64:(tb + 1) * 64], TPB[:, tb * 64:(tb + 1) * 64],
                STAT[:, 119:120], None, ALU.mult),
                reads=["TPiv", "TPB"], writes=["TPB"])
            sch.emit("pe", lambda t, tb=tb: t.matmul(
                PS[6][0:1, 0:64], E16[:, tb:tb + 1], TPB[:, tb * 64:(tb + 1) * 64],
                start=(tb == 0), stop=(tb == 15)),
                reads=["E16", "TPB"], writes=["ps6"] if tb in (0, 15) else [])
        RROW = STAT[0:1, 124:188]
        sch.emit("dve", lambda v: v.tensor_copy(RROW, PS[6][0:1, 0:64]),
                 reads=["ps6"], writes=["RROW", "STAT"])
        sch.dma_emit("pool", lambda g: g.dma_start(out=HALO_IN.ap()[0:64, 1:2], in_=RROW, allow_slow_non_contiguous=True),
                     reads=["RROW"], writes=["HIN"])
        sch.dma_emit("pool", lambda g: g.dma_start(out=HALO_IN.ap()[64:66, 1:2], in_=KMX[0:1, 4:6], allow_slow_non_contiguous=True),
                     reads=["L1mz", "L1z"], writes=["HIN"])
        sch.cc_emit(lambda g: g.collective_compute(
            "AllGather", ALU.bypass, replica_groups=[[0, 1], [2, 3], [4, 5], [6, 7]],
            ins=[HALO_IN.ap()], outs=[HALO_OUT.ap()]),
            reads=["HIN"], writes=["HOUT"])
        R0 = STAT[0:1, 188:252]
        R1 = STAT[0:1, 252:316]
        MZ = STAT[0:1, 316:320]
        sch.dma_emit("pool", lambda g: g.dma_start(out=R0, in_=HALO_OUT.ap()[0:64, 1:2], allow_slow_non_contiguous=True),
                     reads=["HOUT"], writes=["RROW", "STAT"])
        sch.dma_emit("pool", lambda g: g.dma_start(out=R1, in_=HALO_OUT.ap()[128:192, 1:2], allow_slow_non_contiguous=True),
                     reads=["HOUT"], writes=["RROW", "STAT"])
        sch.dma_emit("pool", lambda g: g.dma_start(out=MZ[0:1, 0:2], in_=HALO_OUT.ap()[64:66, 1:2], allow_slow_non_contiguous=True),
                     reads=["HOUT"], writes=["RROW", "STAT"])
        sch.dma_emit("pool", lambda g: g.dma_start(out=MZ[0:1, 2:4], in_=HALO_OUT.ap()[192:194, 1:2], allow_slow_non_contiguous=True),
                     reads=["HOUT"], writes=["RROW", "STAT"])
        C = STAT[0:1, 320:400]
        sch.emit("dve", lambda v: v.tensor_tensor(C[:, 0:1], MZ[:, 0:1], MZ[:, 2:3], ALU.max),
                 reads=["RROW"], writes=["CMB", "STAT"])
        sch.emit("dve", lambda v: v.tensor_scalar(C[:, 1:2], MZ[:, 0:1], C[:, 0:1], None, ALU.subtract),
                 reads=["CMB"], writes=["CMB"])
        sch.emit("dve", lambda v: v.tensor_scalar(C[:, 2:3], MZ[:, 2:3], C[:, 0:1], None, ALU.subtract),
                 reads=["CMB"], writes=["CMB"])
        sch.emit("act", lambda a: a.activation(C[:, 1:3], C[:, 1:3], AF.Exp),
                 reads=["CMB"], writes=["CMB"])
        NWU = STAT[0:1, 400:464]
        sch.emit("dve", lambda v: v.tensor_scalar(NWU, R0, C[:, 1:2], None, ALU.mult),
                 reads=["CMB"], writes=["NWU", "STAT", "SUPD"])
        sch.emit("dve", lambda v: v.tensor_scalar(C[:, 8:72], R1, C[:, 2:3], None, ALU.mult),
                 reads=["CMB"], writes=["CMB"])
        sch.emit("dve", lambda v: v.tensor_tensor(NWU, NWU, C[:, 8:72], ALU.add),
                 reads=["NWU", "CMB"], writes=["NWU"])
        sch.emit("dve", lambda v: v.tensor_scalar(C[:, 3:4], MZ[:, 1:2], C[:, 1:2], None, ALU.mult),
                 reads=["CMB"], writes=["CMB"])
        sch.emit("dve", lambda v: v.tensor_scalar(C[:, 4:5], MZ[:, 3:4], C[:, 2:3], None, ALU.mult),
                 reads=["CMB"], writes=["CMB"])
        sch.emit("dve", lambda v: v.tensor_tensor(C[:, 5:6], C[:, 3:4], C[:, 4:5], ALU.add),
                 reads=["CMB"], writes=["CMB"])
        sch.emit("dve", lambda v: v.tensor_reduce(C[:, 6:7], NWU, AX.X, ALU.add),
                 reads=["NWU"], writes=["CMB"])
        sch.emit("dve", lambda v: v.tensor_scalar(C[:, 7:8], C[:, 5:6], 1e-8, None, ALU.mult),
                 reads=["CMB"], writes=["CMB"])
        sch.emit("dve", lambda v: v.tensor_tensor(C[:, 7:8], C[:, 7:8], C[:, 6:7], ALU.add),
                 reads=["CMB"], writes=["CMB"])
        sch.emit("act", lambda a: a.activation(C[:, 7:8], C[:, 7:8], AF.Ln), reads=["CMB"], writes=["CMB"])
        sch.emit("act", lambda a: a.activation(C[:, 7:8], C[:, 7:8], AF.Exp, scale=-1.0),
                 reads=["CMB"], writes=["CMB"])
        sch.emit("dve", lambda v: v.tensor_scalar(NWU, NWU, C[:, 7:8], None, ALU.mult),
                 reads=["NWU", "CMB"], writes=["NWU"])
        W64 = DECB[:, 0:128].bitcast(F32)
        bcast128(W64[:, :], NWU, "NWU", "W64")
        sch.buf.setdefault("DECB", {"w": None, "r": []})
        sch.buf["DECB"] = {"w": sch.buf["W64"]["w"], "r": []}
        if L1_STOP == 2:
            sch.serialize = False
            return
        SCC = CREP[:, 0:1024]
        sch.emit("dve", lambda v: v.memset(SCC[:, :], 0.0),
                 writes=["SCC", "CREP", "XR6", "XR7", "CREPb"])
        sch.serialize = False
        for n in range(64):
            half = MA[:, (n % 2) * 1024:(n % 2) * 1024 + 1024]
            sch.dma_emit("pool", lambda s, n=n, half=half: s.dma_start(
                out=half.rearrange("p (a c) -> p a c", a=NCH),
                in_=L1CMP[n * D:(n + 1) * D, :].rearrange("(a p) c -> p a c", p=128)),
                writes=[f"CB{n % 2}"] + (["MAg0", "MAg1", "MAg2", "MAg3", "K16",
                                          "CREP2", "MXR"] if n < 2 else []))
            sch.emit("dve", lambda v, n=n, half=half: v.scalar_tensor_tensor(
                SCC[:, :], half, W64[:, n:n + 1], SCC[:, :], ALU.mult, ALU.add),
                reads=[f"CB{n % 2}", "W64", "SCC"], writes=["SCC"])
        SC16 = XR16[:, 8192:9216]
        sch.emit("dve", lambda v: v.tensor_copy(SC16[:, :], SCC[:, :]),
                 reads=["SCC"], writes=["SC16", "SREP", "XR4", "XR5", "SREPb", "KSTG2b"])
        QFM = RSTD
        for g in range(4):
            for cf in range(NCH):
                sch.emit("pe", lambda t, g=g, cf=cf: t.matmul(
                    PS[g][:, 0:512], SC16[:, cf * 128:(cf + 1) * 128],
                    XN16[:, cf * T + g * 512: cf * T + (g + 1) * 512],
                    start=(cf == 0), stop=(cf == NCH - 1)),
                    reads=[f"XN{cf}", "SC16"],
                    writes=[f"ps{g}"] if cf in (0, NCH - 1) else [])
        for g in range(4):
            sch.emit("dve", lambda v, g=g: v.tensor_scalar(
                QFM[:, g * 512:(g + 1) * 512], PS[g][:, 0:512], 128.0 ** -0.5, None, ALU.mult),
                reads=[f"ps{g}"],
                writes=["QFM"] + (["RSg0", "RSg1", "RSg2", "RSg3", "MXR2", "Q16"] if g == 0 else []))
        for kc in range(32):
            buf = WSL if kc % 2 == 0 else W2SL
            sl = buf[:, 0:1024]
            sch.dma_emit("pool", lambda s, kc=kc, sl=sl: s.dma_start(
                out=sl, in_=L1KT[:, kc * 1024:(kc + 1) * 1024]),
                writes=[f"KTG{kc % 2}"] + (["WSL0", "WSL1", "W2SL0", "W2SL1",
                                            "VTMB", "MW1S"] if kc < 2 else []))
            sch.dma_emit("act", lambda a, kc=kc, sl=sl: a.dma_start(
                out=KTB.ap()[:, kc * 1024:(kc + 1) * 1024], in_=sl),
                reads=[f"KTG{kc % 2}"], writes=["KTB"])
        if L1_STOP == 3:
            sch.serialize = False
            return
        IOT = MA[:, 0:2048].bitcast(F32)
        P256S = KQ[:, 9:10]
        sch.dma_emit("sp", lambda s: s.dma_start(out=IOT[:, :], in_=IOTA1K),
                     writes=["IOT", "CB0", "CB1", "MAg0", "MAg1", "MAg2", "MAg3",
                             "K16", "MXR", "CREP2"])
        sch.dma_emit("sp", lambda s: s.dma_start(out=P256S[:, :], in_=P256),
                     writes=["P256", "KQr", "KQz"])
        IDXU = STAT[:, 460:468].bitcast(U32)
        MX8a = SCB[:, 0:8]
        MX8b = SCB[:, 8:16]
        IX8a = SCB[:, 16:32].bitcast(U32)
        IX8b = SCB[:, 32:48].bitcast(U32)
        OUTB = GTMB[:, 0:2048].bitcast(F32)
        OUT16 = DECB[:, 1024:2048]
        first_tb = [True]
        for tb in range(16):
            for kc in range(64):
                slot = WSL[:, (kc % 4) * 512:(kc % 4) * 512 + 512]
                sch.dma_emit("sp", lambda s, kc=kc, slot=slot: s.dma_start(
                    out=slot, in_=KTB.ap()[:, kc * 512:(kc + 1) * 512]),
                    reads=["KTB"],
                    writes=[f"KTS{kc % 4}"] + (["KTG0", "KTG1"] if tb == 0 and kc < 4 else []),
                    extra=[("c", "pe", sch.cnt["pe"])])
                sch.emit("pe", lambda t, tb=tb, kc=kc, slot=slot: t.matmul(
                    PS[kc % 4][:, 0:512], QFM[:, tb * 128:(tb + 1) * 128], slot,
                    start=True, stop=True),
                    reads=["QFM", f"KTS{kc % 4}"], writes=[f"ps{kc % 4}"])
                dst = (XR16[:, kc * 512:(kc + 1) * 512] if kc < 32 else
                       XN16[:, (kc - 32) * 512:(kc - 32) * 512 + 512])
                wr = [f"SCe{kc % 8}"]
                if first_tb[0] and kc == 0:
                    wr += XR_KEYS + XN_KEYS + ["SC16", "SCC", "UB"] + HQ_KEYS
                eng = "dve" if kc % 2 == 0 else "act"
                if eng == "dve":
                    sch.emit("dve", lambda v, dst=dst, kc=kc: v.tensor_copy(
                        dst, PS[kc % 4][:, 0:512]),
                        reads=[f"ps{kc % 4}"], writes=wr)
                else:
                    sch.emit("act", lambda a, dst=dst, kc=kc: a.copy(
                        dst, PS[kc % 4][:, 0:512]),
                        reads=[f"ps{kc % 4}"], writes=wr)
            first_tb[0] = False
            if L1_STOP == 4:
                continue
            sch.serialize = True
            sch.serialize = True
            sch.dma_emit("act", lambda a, tb=tb: a.dma_start(
                out=SCD.ap().rearrange("(t s) q -> t (s q)", t=128)[:, 0:16384],
                in_=XR16[:, :]),
                reads=[f"SCe{i}" for i in range(8)], writes=["SCD"])
            sch.dma_emit("act", lambda a, tb=tb: a.dma_start(
                out=SCD.ap().rearrange("(t s) q -> t (s q)", t=128)[:, 16384:32768],
                in_=XN16[:, :]),
                reads=[f"SCe{i}" for i in range(8)], writes=["SCD"])
            SEGW = STAT[:, 100:356]
            sch.emit("dve", lambda v: v.tensor_reduce(
                SEGW[:, 0:128], XR16[:, :].rearrange("p (s q) -> p s q", s=128), AX.X, ALU.max),
                reads=[f"SCe{i}" for i in range(8)], writes=["STAT"])
            sch.emit("dve", lambda v: v.tensor_reduce(
                SEGW[:, 128:256], XN16[:, :].rearrange("p (s q) -> p s q", s=128), AX.X, ALU.max),
                reads=[f"SCe{i}" for i in range(8)], writes=["STAT"])
            if L1_STOP == 6:
                sch.serialize = False
                continue
            XSC = XR16[:, :].bitcast(F32)
            SEL = XSC[:, 0:256]
            T1 = XSC[:, 256:512]
            IOTB = XSC[:, 512:1536]
            TFP = XSC[:, 1536:2560]
            SEGG = XN16[:, 0:1024]
            SELU = XN16[:, 1024:2048]
            SEGIDS = STAT[:, 60:68]
            RM = STAT[:, 48:49]
            sch.emit("dve", lambda v: v.tensor_scalar(IOTB, IOT[:, :], -1e9, None, ALU.add),
                     reads=["IOT"], writes=["XSCR"] + XR_KEYS)
            for j in range(8):
                sch.emit("dve", lambda v: v.tensor_reduce(RM, SEGW, AX.X, ALU.max),
                         reads=["STAT"], writes=["STAT"])
                sch.emit("dve", lambda v: v.tensor_scalar(SEL, SEGW, RM, None, ALU.is_equal),
                         reads=["STAT", "XSCR"], writes=["XSCR"])
                sch.emit("dve", lambda v: v.tensor_tensor(T1, SEL, IOTB[:, 0:256], ALU.mult),
                         reads=["XSCR"], writes=["XSCR"])
                sch.emit("dve", lambda v: v.tensor_scalar(T1, T1, 1e9, None, ALU.add),
                         reads=["XSCR"], writes=["XSCR"])
                sch.emit("dve", lambda v, j=j: v.tensor_reduce(
                    SEGIDS[:, j:j + 1], T1, AX.X, ALU.min),
                    reads=["XSCR"], writes=["STAT"])
                sch.emit("dve", lambda v, j=j: v.tensor_scalar(
                    SEL, IOT[:, 0:256], SEGIDS[:, j:j + 1], None, ALU.is_equal),
                    reads=["IOT", "STAT"], writes=["XSCR"])
                sch.emit("dve", lambda v: v.scalar_tensor_tensor(
                    SEGW, SEL, -1e30, SEGW, ALU.mult, ALU.add),
                    reads=["XSCR", "STAT"], writes=["STAT"])
            if L1_STOP == 7:
                sch.serialize = False
                continue
            ROWF = STAT[:, 68:76]
            ROWU = STAT[:, 76:84].bitcast(U32)
            for j in range(8):
                sch.emit("dve", lambda v, j=j: v.tensor_scalar(
                    ROWF[:, j:j + 1], SEGIDS[:, j:j + 1], P256S[:, 0:1], None, ALU.add),
                    reads=["STAT", "P256"], writes=["STAT"])
            sch.emit("dve", lambda v: v.tensor_scalar(ROWF, ROWF, 0.0, 32767.0, ALU.max, ALU.min),
                     reads=["STAT"], writes=["STAT"])
            sch.emit("dve", lambda v: v.tensor_copy(ROWU[:, :], ROWF), reads=["STAT"], writes=["STAT"])
            for j in range(8):
                sch.dma_emit("pool", lambda g, j=j: g.indirect_dma_start(
                    out=SEGG[:, j * 128:(j + 1) * 128], out_offset=None,
                    in_=SCD.ap(), in_offset=bass.IndirectOffsetOnAxis(ap=ROWU[:, j:j + 1], axis=0)),
                    reads=["SCD", "STAT"], writes=["SEGG"] + XN_KEYS)
            if L1_STOP == 8:
                sch.serialize = False
                continue
            M8B = SCB[:, 0:8]
            M8F = STAT[:, 0:8]
            sch.emit("dve", lambda v: v.max(M8B, SEGG), reads=["SEGG"], writes=["SCB", "E16"])
            sch.emit("dve", lambda v: v.tensor_copy(M8F, M8B), reads=["SCB"], writes=["STAT"])
            NEGM = STAT[:, 49:50]
            W8 = STAT[:, 41:49]
            WN8 = STAT[:, 52:60]
            sch.emit("dve", lambda v: v.tensor_scalar(NEGM, M8F[:, 0:1], -1.0, None, ALU.mult),
                     reads=["STAT"], writes=["STAT"])
            sch.emit("act", lambda a: a.activation(W8, M8F, AF.Exp, bias=NEGM),
                     reads=["STAT"], writes=["STAT"])
            sch.emit("dve", lambda v: v.tensor_reduce(STAT[:, 50:51], W8, AX.X, ALU.add),
                     reads=["STAT"], writes=["STAT"])
            sch.emit("act", lambda a: a.activation(STAT[:, 51:52], STAT[:, 50:51], AF.Ln),
                     reads=["STAT"], writes=["STAT"])
            sch.emit("act", lambda a: a.activation(STAT[:, 51:52], STAT[:, 51:52], AF.Exp, scale=-1.0),
                     reads=["STAT"], writes=["STAT"])
            sch.emit("dve", lambda v: v.tensor_scalar(WN8, W8, STAT[:, 51:52], None, ALU.mult),
                     reads=["STAT"], writes=["STAT"])
            POSI = STAT[:, 8:16]
            KEYF = STAT[:, 16:24]
            KEYU = STAT[:, 24:32].bitcast(U32)
            for i in range(8):
                sch.emit("dve", lambda v, i=i: v.tensor_scalar(
                    SELU, SEGG, M8F[:, i:i + 1], None, ALU.is_equal),
                    reads=["SEGG", "STAT"], writes=["SELU"] + XN_KEYS)
                sch.emit("dve", lambda v: v.tensor_tensor(TFP, SELU, IOTB, ALU.mult),
                         reads=["SELU", "XSCR"], writes=["XSCR"])
                sch.emit("dve", lambda v: v.tensor_scalar(TFP, TFP, 1e9, None, ALU.add),
                         reads=["XSCR"], writes=["XSCR"])
                sch.emit("dve", lambda v, i=i: v.tensor_reduce(
                    POSI[:, i:i + 1], TFP, AX.X, ALU.min),
                    reads=["XSCR"], writes=["STAT"])
                sch.emit("dve", lambda v, i=i: v.tensor_scalar(
                    STAT[:, 33:34], POSI[:, i:i + 1], 1.0 / 128.0, -0.4999, ALU.mult, ALU.add),
                    reads=["STAT"], writes=["STAT"])
                sch.emit("dve", lambda v: v.tensor_scalar(
                    STAT[:, 33:34], STAT[:, 33:34], RBIG, -RBIG, ALU.add, ALU.add),
                    reads=["STAT"], writes=["STAT"])
                sch.emit("dve", lambda v, i=i: v.scalar_tensor_tensor(
                    STAT[:, 34:35], STAT[:, 33:34], -128.0, POSI[:, i:i + 1], ALU.mult, ALU.add),
                    reads=["STAT"], writes=["STAT"])
                sch.emit("dve", lambda v: v.tensor_scalar(
                    STAT[:, 35:43] if False else STAT[:, 84:92], IOT[:, 0:8], STAT[:, 33:34], None, ALU.is_equal),
                    reads=["IOT", "STAT"], writes=["STAT"])
                sch.emit("dve", lambda v: v.tensor_tensor(
                    STAT[:, 92:100], STAT[:, 84:92], SEGIDS, ALU.mult),
                    reads=["STAT"], writes=["STAT"])
                sch.emit("dve", lambda v: v.tensor_reduce(
                    STAT[:, 35:36], STAT[:, 92:100], AX.X, ALU.add),
                    reads=["STAT"], writes=["STAT"])
                sch.emit("dve", lambda v, i=i: v.scalar_tensor_tensor(
                    KEYF[:, i:i + 1], STAT[:, 35:36], 128.0, STAT[:, 34:35], ALU.mult, ALU.add),
                    reads=["STAT"], writes=["STAT"])
            sch.emit("dve", lambda v: v.tensor_scalar(KEYF, KEYF, 0.0, 32767.0, ALU.max, ALU.min),
                     reads=["STAT"], writes=["STAT"])
            sch.emit("dve", lambda v: v.tensor_copy(KEYU[:, :], KEYF), reads=["STAT"], writes=["STAT"])
            if L1_STOP == 5:
                sch.serialize = False
                continue
            GB16 = HQB[:, :].bitcast(BF16)
            for j in range(8):
                sch.dma_emit("pool", lambda g, j=j: g.indirect_dma_start(
                    out=GB16[:, j * 1024:(j + 1) * 1024], out_offset=None,
                    in_=L1V, in_offset=bass.IndirectOffsetOnAxis(ap=KEYU[:, j:j + 1], axis=0)),
                    reads=["STAT"], writes=[f"GB{j}"] + (["UB"] if j == 0 else []))
            for j in range(8):
                if j == 0:
                    sch.emit("dve", lambda v: v.tensor_scalar(
                        OUTB[:, 0:1024], GB16[:, 0:1024], WN8[:, 0:1], None, ALU.mult),
                        reads=["GB0", "STAT", "TPB"], writes=["OUTB", "GTMB", "HSx", "OGFM", "LB", "TPB"])
                else:
                    sch.emit("dve", lambda v, j=j: v.scalar_tensor_tensor(
                        OUTB[:, 0:1024], GB16[:, j * 1024:(j + 1) * 1024], WN8[:, j:j + 1],
                        OUTB[:, 0:1024], ALU.mult, ALU.add),
                        reads=[f"GB{j}", "STAT", "OUTB"], writes=["OUTB"])
            sch.emit("dve", lambda v: v.tensor_copy(OUT16[:, :], OUTB[:, 0:1024]),
                     reads=["OUTB"], writes=["OUT16", "CV1", "HS16", "S2R16", "AB2", "SRC16"])
            for dc in range(NCH):
                sch.emit("pe", lambda t, dc=dc: t.transpose(
                    PS[7].bitcast(BF16)[:, 0:128], OUT16[:, dc * 128:(dc + 1) * 128], IDEN[:, :]),
                    reads=["OUT16", "IDEN"], writes=["ps7"])
                sch.emit("dve", lambda v, dc=dc, tb=tb: v.tensor_tensor(
                    X[:, dc * T + tb * 128: dc * T + tb * 128 + 128],
                    X[:, dc * T + tb * 128: dc * T + tb * 128 + 128],
                    PS[7].bitcast(BF16)[:, 0:128], ALU.add),
                    reads=["ps7", f"X{dc}"], writes=[f"X{dc}"])
            sch.serialize = False
        sch.serialize = False

    if True:
        for buf, nm in [(X, "X"), (XR16, "XR"), (XN16, "XN"), (RSTD, "RS"), (MA, "MAz"),
                        (STAT, "STATz"), (WBUF, "Wz"), (KSTG, "KSz"), (TMPR, "TMPRz"),
                        (KQ, "KQz"), (KSC, "KSCz")]:
            sch.emit("dve", lambda v, buf=buf: v.memset(buf[:, :], 0.0), writes=[nm + "init"])
    for c in range(NCH):
        sch.dma_emit("sp", lambda s, c=c: s.dma_start(out=chunk(X, c), in_=XIN[c * 128:(c + 1) * 128, :]),
                     writes=[f"X{c}"], extra=[("c", "dve", sch.cnt["dve"])])
    sch.dma_emit("sp", lambda s: s.dma_start(out=ONES[:, :], in_=ONES128), writes=["ONES"])
    sch.dma_emit("sp", lambda s: s.dma_start(out=ONESF_S[:, :], in_=ONESF), writes=["ONESF"])
    sch.dma_emit("sp", lambda s: s.dma_start(out=FLG[:, :], in_=FLAGS), writes=["FLG"])

    if N_SUB >= 1:
        l0ffn_prep()
        l0ffn()
    if N_SUB >= 2:
        l0attn()
    if N_SUB >= 3:
        if N_SUB >= 2:
            l0ffn_prep()
        l0ffn()
    if N_SUB >= 4:
        l1ffn_prep()
        sch.dma_emit("sp", lambda s: s.dma_start(out=CWP[:, :], in_=CWPACK), writes=["CWP", "S16D"])
        l1ffn(0)
    if N_SUB >= 5:
        l1attn()
    if N_SUB >= 6:
        l1ffn(1)

    for c in range(NCH):
        sch.dma_emit("sp", lambda s, c=c: s.dma_start(out=XOUT[c * 128:(c + 1) * 128, :], in_=chunk(X, c)),
                     reads=[f"X{c}"], writes=[f"XO{c}"])

    if DEBUG:
        DXN = nc.dram_tensor("DXN", [D, T], BF16, kind="ExternalOutput").ap()
        DRS = nc.dram_tensor("DRS", [128, T], BF16, kind="ExternalOutput").ap()
        DMA_ = nc.dram_tensor("DMA_", [128, T], BF16, kind="ExternalOutput").ap()
        DWQ = nc.dram_tensor("DWQ", [128, 8 * 1024], BF16, kind="ExternalOutput").ap()
        DTMP = nc.dram_tensor("DTMP", [16, T], BF16, kind="ExternalOutput").ap()
        for c in range(NCH):
            sch.dma_emit("sp", lambda s, c=c: s.dma_start(out=DXN[c * 128:(c + 1) * 128, :], in_=chunk(XN16, c)),
                         reads=[f"XN{c}"], writes=[])
        sch.dma_emit("sp", lambda s: s.dma_start(out=DRS, in_=RSTD[:, :]), reads=[f"RSg{g}" for g in range(NTG)])
        sch.dma_emit("sp", lambda s: s.dma_start(out=DMA_, in_=MA[:, :]), reads=[f"MAg{g}" for g in range(NTG)])
        sch.dma_emit("sp", lambda s: s.dma_start(out=DWQ, in_=WBUF[:, :]), reads=[f"W{c}" for c in range(NCH)])
        sch.dma_emit("sp", lambda s: s.dma_start(out=DTMP, in_=TMPR[:, :]), reads=[f"TMPR{g}" for g in range(NTG)])
        DKSC = nc.dram_tensor("DKSC", [128, 8], F32, kind="ExternalOutput").ap()
        DKMX = nc.dram_tensor("DKMX", [1, 8], F32, kind="ExternalOutput").ap()
        sch.dma_emit("sp", lambda s: s.dma_start(out=DKMX, in_=KMX[:, :]), reads=["KM"])
        DKQ = nc.dram_tensor("DKQ", [128, 12], F32, kind="ExternalOutput").ap()
        sch.dma_emit("sp", lambda s: s.dma_start(out=DKSC, in_=KSC[:, :]), reads=["KM", "KB"])
        sch.dma_emit("sp", lambda s: s.dma_start(out=DKQ, in_=KQ[:, :]), reads=["KQr"])
        if N_SUB >= 4:
            DXQ = nc.dram_tensor("DXQ", [D, T], BF16, kind="ExternalOutput").ap()
            DSREP = nc.dram_tensor("DSREP", [128, T], F32, kind="ExternalOutput").ap()
            DCREP = nc.dram_tensor("DCREP", [128, T], F32, kind="ExternalOutput").ap()
            DKSC2 = nc.dram_tensor("DKSC2", [128, 8], F32, kind="ExternalOutput").ap()
            DHQ = nc.dram_tensor("DHQ", [128, 32 * 512], mybir.dt.float16, kind="ExternalOutput").ap()
            DC2 = nc.dram_tensor("DC2", [128, 512], F32, kind="ExternalOutput").ap()
            for c in range(NCH):
                sch.dma_emit("sp", lambda s, c=c: s.dma_start(out=DXQ[c * 128:(c + 1) * 128, :], in_=chunk(XN16, c)),
                             reads=[f"XN{c}"])
            sch.dma_emit("sp", lambda s: s.dma_start(out=DSREP, in_=SREP[:, :]), reads=["SREP"])
            sch.dma_emit("sp", lambda s: s.dma_start(out=DCREP, in_=CREP[:, :]), reads=["CREP"])
            sch.dma_emit("sp", lambda s: s.dma_start(out=DKSC2, in_=KSC2[:, :]), reads=["MWb0", "MWb1", "MWi0", "MWi1"])
            sch.dma_emit("sp", lambda s: s.dma_start(out=DHQ, in_=HQB[:, :]), reads=[f"HQ{ch}" for ch in range(32)])
            sch.dma_emit("sp", lambda s: s.dma_start(out=DC2, in_=CREP2[:, :]), reads=["CREP2"])

    def runner(eng_name):
        def run(engobj):
            for waits, fn, inc in sch.q[eng_name]:
                for (kind, weng), val in waits.items():
                    sem = sem_c[weng] if kind == "c" else (sem_d[weng] if kind == "d" else sem_cc)
                    engobj.wait_ge(sem, val)
                inst = fn(engobj)
                if inst is not None:
                    kind, amt = inc
                    sem = sem_c[eng_name] if kind == "c" else (sem_d[eng_name] if kind == "d" else sem_cc)
                    inst.then_inc(sem, amt)
            if eng_name == "sp":
                engobj.wait_ge(sem_d["sp"], 16 * sch.dma["sp"])
        return run

    block.tensor(runner("pe"))
    block.vector(runner("dve"))
    block.scalar(runner("act"))
    block.gpsimd(runner("pool"))
    block.sync(runner("sp"))

    stack.close()
    return nc


def _prep_inputs(inputs):
    x = np.asarray(inputs["L0_attn_hidden_states"], dtype=np.float32)
    ones = np.ones((128, 128), dtype=ml_dtypes.bfloat16)
    onesf = np.ones((1, 128), dtype=np.float32)
    l0k = np.ascontiguousarray(inputs["L0_ffn_kernel"], dtype=np.float32)
    l0a = np.ascontiguousarray(inputs["L0_ffn_a"], dtype=np.float32)
    l0b = np.ascontiguousarray(inputs["L0_ffn_b"], dtype=np.float32)
    l0bias = np.ascontiguousarray(
        np.asarray(inputs["L0_ffn_bias"], dtype=np.float32).reshape(NCH, 128).T)
    w1 = np.asarray(inputs["L1_ffn_w1"], dtype=np.float32)
    w2 = np.asarray(inputs["L1_ffn_w2"], dtype=np.float32)
    w1p = np.ascontiguousarray(
        w1.reshape(32, 128, 8, 128).transpose(0, 3, 2, 1).reshape(4096, 1024))
    w2t = np.ascontiguousarray(w2.T)
    cw = np.asarray(inputs["L1_ffn_conv_w"], dtype=np.float32).reshape(4096, 3)
    cb = np.asarray(inputs["L1_ffn_conv_b"], dtype=np.float32)
    cwpack = np.zeros((128, 128), dtype=np.float32)
    for ch in range(32):
        cwpack[:, 4 * ch + 0] = cw[ch * 128:(ch + 1) * 128, 0]
        cwpack[:, 4 * ch + 1] = cw[ch * 128:(ch + 1) * 128, 1]
        cwpack[:, 4 * ch + 2] = cw[ch * 128:(ch + 1) * 128, 2]
        cwpack[:, 4 * ch + 3] = cb[ch * 128:(ch + 1) * 128]
    def col8(v):
        return np.ascontiguousarray(np.asarray(v, np.float32).reshape(-1).reshape(NCH, 128).T)
    tmx8 = col8(inputs["L0_attn_time_maa_x"])
    tdec = col8(inputs["L0_attn_time_decay"])
    bgrow = np.asarray(inputs["L0_attn_bg"], np.float32).reshape(1, D)
    mw1 = np.ascontiguousarray(np.asarray(inputs["L0_attn_time_maa_w1"], np.float32))
    mw2 = np.asarray(inputs["L0_attn_time_maa_w2"], np.float32)
    mw2x = np.zeros((165, D), dtype=np.float32)
    tm_order = ["r", "k", "v", "w", "g"]
    for f in range(5):
        mw2x[f * 33:f * 33 + 32] = mw2[f]
        mw2x[f * 33 + 32] = np.asarray(inputs[f"L0_attn_time_maa_{tm_order[f]}"], np.float32).reshape(-1)
    dw1 = np.ascontiguousarray(np.asarray(inputs["L0_attn_time_decay_w1"], np.float32))
    dw2 = np.ascontiguousarray(np.asarray(inputs["L0_attn_time_decay_w2"], np.float32))
    wqkv = np.ascontiguousarray(np.concatenate([
        np.asarray(inputs["L0_attn_Wq"], np.float32),
        np.asarray(inputs["L0_attn_Wk"], np.float32),
        np.asarray(inputs["L0_attn_Wv"], np.float32),
        np.asarray(inputs["L0_attn_Wg"], np.float32),
        np.asarray(inputs["L0_attn_Wo"], np.float32)], axis=0))
    causal = np.tril(np.ones((128, 128))).T.astype(ml_dtypes.bfloat16)
    iden = np.eye(128, dtype=ml_dtypes.bfloat16)
    l1a = np.ascontiguousarray(np.asarray(inputs["L1_attn_A"], np.float32))
    l1at = np.ascontiguousarray(l1a.T)
    l1bm = np.ascontiguousarray(np.asarray(inputs["L1_attn_Bm"], np.float32))
    l1wit = np.ascontiguousarray(np.asarray(inputs["L1_attn_W_imp"], np.float32).T)
    l1wrt = np.ascontiguousarray(np.asarray(inputs["L1_attn_W_router"], np.float32).T)
    l1cmp = np.ascontiguousarray(
        np.asarray(inputs["L1_attn_compress"], np.float32).reshape(64 * 1024, 128))
    l1kt = np.ascontiguousarray(np.asarray(inputs["L1_attn_K"], np.float32).T)
    l1v = np.ascontiguousarray(np.asarray(inputs["L1_attn_V"], np.float32))
    iota1k = np.ascontiguousarray(np.tile(np.arange(1024, dtype=np.float32), (128, 1)))
    p256 = (np.arange(128, dtype=np.float32) * 256.0).reshape(128, 1).copy()
    in_maps = []
    for i in range(8):
        b, r = i // 2, i % 2
        xs = np.ascontiguousarray(x[b, r * T:(r + 1) * T, :].T)
        flags = np.zeros((128, 8), dtype=np.float32)
        flags[:, 0] = float(r)
        flags[:, 1] = float(1 - r)
        flags[:, 2] = EPS
        flags[:, 3] = 1.0
        flags[:, 4] = float(np.log(127.0))
        in_maps.append(dict(
            XIN=xs, FLAGS=flags, ONES128=ones, ONESF=onesf,
            L0K=l0k, L0A=l0a, L0B=l0b, L0BIAS=l0bias,
            W1P=w1p, W2T=w2t, CWPACK=cwpack,
            TMX8=tmx8, TDEC=tdec, BGROW=bgrow, MW1=mw1, MW2X=mw2x,
            DW1=dw1, DW2=dw2, WQKV=wqkv, CAUSAL=causal, IDEN128=iden,
            L1A=l1a, L1AT=l1at, L1BM=l1bm, L1WIT=l1wit, L1WRT=l1wrt,
            L1CMP=l1cmp, L1KT=l1kt, L1V=l1v, IOTA1K=iota1k, P256=p256,
        ))
    return in_maps


def kernel(**inputs):
    nc = build_program()
    nc.compile()
    in_maps = _prep_inputs(inputs)
    res = run_bass_kernel_spmd(nc, in_maps, core_ids=list(range(8)))
    out = np.zeros((4, 4096, D), dtype=np.float32)
    for i in range(8):
        b, r = i // 2, i % 2
        out[b, r * T:(r + 1) * T, :] = res.results[i]["XOUT"].T
    return out
